# version 12
# speedup vs baseline: 1.5377x; 1.0016x over previous
import numpy as np
import ml_dtypes

import concourse.bass as bass
import concourse.mybir as mybir
from concourse.bass_utils import run_bass_kernel_spmd

F32 = mybir.dt.float32
BF16 = mybir.dt.bfloat16
AX = mybir.AxisListType.X
ALU = mybir.AluOpType
ACT = mybir.ActivationFunctionType

B, S, D = 4, 4096, 1024
NH, HD, DROPE, LAT, DH = 16, 64, 16, 256, 4096
NPAIR = NH // 2
QTN = 17
LQ = QTN * 128
SKT = S // 128
MAGIC = float(np.float32(1.5 * 2.0 ** 23))
EXP_SHIFT = 25.0
LN127 = float(np.log(127.0))
QBLOCKS = [(0, 512), (512, 512), (1024, 512), (1536, 512), (2048, 128)]
FFN_CHUNKS = [(k * 256, 256) for k in range(8)] + [(2048, 128)]

_TAPS = []
_PREFIX = None

ENGS = ("tensor", "scalar", "vector", "gpsimd", "sync")


class Step:
    __slots__ = ("eng", "sem", "fn", "waits", "n_inc", "self_inc", "count",
                 "pairs", "slow")

    def __init__(self, eng, sem, fn, waits, n_inc, self_inc):
        self.eng, self.sem, self.fn = eng, sem, fn
        self.waits, self.n_inc, self.self_inc = list(waits), n_inc, self_inc
        self.count = None
        self.pairs = None
        self.slow = False


class Sched:
    def __init__(self):
        self.steps = []

    def add(self, eng, sem, fn, waits=(), n_inc=1, self_inc=False):
        if isinstance(fn, tuple) and fn[0] == "dma":
            st = Step(eng, sem, None, [w for w in waits if w is not None],
                      len(fn[1]), self_inc)
            st.pairs = fn[1]
            st.slow = (len(fn) > 2 and fn[2])
        else:
            st = Step(eng, sem, fn, [w for w in waits if w is not None],
                      n_inc, self_inc)
        self.steps.append(st)
        return st

    def plan(self):
        counts = {}
        for st in self.steps:
            counts[st.sem] = counts.get(st.sem, 0) + st.n_inc
            st.count = counts[st.sem]
        self.sem_names = sorted(counts)

    def emit(self, pr, blk):
        sems = {}
        for name in self.sem_names:
            g = pr.nc.semaphore("sem_" + name)
            sems[name] = g.__enter__()
            pr.ctx.append(g)
        by_eng = {e: [st for st in self.steps if st.eng == e] for e in ENGS}

        def mk(engname):
            def body(eng):
                for st in by_eng[engname]:
                    seen = {}
                    for w in st.waits:
                        ws, wc = w.sem, w.count
                        if seen.get(ws, -1) >= wc:
                            continue
                        seen[ws] = wc
                        v = wc * (16 if ws.startswith("d") else 1)
                        if v > 0:
                            eng.wait_ge(sems[ws], v)
                    if st.pairs is not None:
                        for dst, srcp in st.pairs:
                            i = (eng.dma_start(dst, srcp,
                                               allow_slow_non_contiguous=True)
                                 if st.slow else eng.dma_start(dst, srcp))
                            i.then_inc(sems[st.sem], 16)
                    else:
                        last = st.fn(eng)
                        if not st.self_inc:
                            assert last is not None, (st.sem, st.eng)
                            last.then_inc(sems[st.sem],
                                          16 * st.n_inc if st.sem.startswith("d")
                                          else st.n_inc)
            return body

        blk.tensor(mk("tensor"))
        blk.scalar(mk("scalar"))
        blk.vector(mk("vector"))
        blk.gpsimd(mk("gpsimd"))
        blk.sync(mk("sync"))


class Prog:
    def __init__(self):
        self.nc = bass.Bass("TRN2", target_bir_lowering=False)
        self.ctx = []

    def S(self, name, shape, dt=F32):
        g = self.nc.sbuf_tensor(name, shape, dt)
        t = g.__enter__()
        self.ctx.append(g)
        return t

    def P(self, name, shape, dt=F32):
        g = self.nc.psum_tensor(name, shape, dt)
        t = g.__enter__()
        self.ctx.append(g)
        return t


def _host_consts():
    identf = np.eye(128, dtype=np.float32)
    identb = np.eye(128, dtype=ml_dtypes.bfloat16)
    onesb = np.ones((128, 128), dtype=ml_dtypes.bfloat16)
    inv = 10000.0 ** (-(np.arange(0, HD, 2, dtype=np.float64)[: DROPE // 2]) / HD)
    rt = np.zeros((NPAIR, 128, 128), dtype=np.float64)
    for p in range(NPAIR):
        for sub, h in enumerate((2 * p, 2 * p + 1)):
            base = sub * 64
            blk = np.eye(64)
            for j in range(DROPE // 2):
                c = np.cos(h * inv[j]); s = np.sin(h * inv[j])
                blk[2 * j, 2 * j] = c;     blk[2 * j, 2 * j + 1] = -s
                blk[2 * j + 1, 2 * j] = s; blk[2 * j + 1, 2 * j + 1] = c
            rt[p, base:base + 64, base:base + 64] = blk.T
    rtf = np.ascontiguousarray(
        rt.astype(np.float32).transpose(1, 0, 2).reshape(128, NPAIR * 128))
    return {"identf": identf, "identb": identb, "onesb": onesb, "rt": rtf}


def build():
    pr = Prog()
    nc = pr.nc
    sch = Sched()

    DI = lambda n, s, dt=F32: nc.dram_tensor(n, s, dt, kind="ExternalInput").ap()
    d_xq = DI("xq", [LQ, D]); d_xf = DI("xf", [S, D])
    d_wq = DI("wq", [D, D]); d_wka = DI("wka", [LAT, D]); d_wkb = DI("wkb", [D, LAT])
    d_wva = DI("wva", [LAT, D]); d_wvb = DI("wvb", [D, LAT]); d_wo = DI("wo", [D, D])
    d_w1 = DI("w1", [DH, D]); d_cw = DI("cw", [DH, 3]); d_cb = DI("cb", [DH])
    d_w2 = DI("w2", [D, DH])
    d_identf = DI("identf", [128, 128]); d_identb = DI("identb", [128, 128], BF16)
    d_onesb = DI("onesb", [128, 128], BF16); d_rt = DI("rt", [128, NPAIR * 128])
    d_y = nc.dram_tensor("y", [LQ, D], F32, kind="ExternalOutput").ap()
    d_x2sp = nc.dram_tensor("x2sp", [LQ, D], F32).ap()
    d_g1sp = nc.dram_tensor("g1sp", [8, 128, LQ], BF16).ap()
    d_qsp = nc.dram_tensor("qsp", [NPAIR, 128, LQ], BF16).ap()
    d_mxsp = nc.dram_tensor("mxsp", [1, LQ], F32).ap()

    taps = {}
    for t in _TAPS:
        shp, dt = {
            "attn": ([LQ, D], F32), "x2": ([LQ, D], F32),
            "kchk": ([128, S], F32), "qchk": ([128, LQ], F32),
            "vchk": ([128, 1024], F32), "den": ([128, 512], F32),
            "g1": ([8, 128, LQ], F32), "a1": ([128, 8192], F32),
            "g2": ([128, 8192], F32), "t1": ([128, 4096], F32),
            "t2": ([128, 1024], F32), "s2": ([128, 256], F32),
            "mx": ([1, LQ], F32), "usb": ([128, 512], F32),
            "yw": ([128, 1024], F32), "stt": ([128, 72], F32),
            "x2w": ([128, 1024], F32),
        }[t]
        taps[t] = nc.dram_tensor("tap_" + t, shp, dt, kind="ExternalOutput").ap()

    A = pr.S("A", [128, 16384], BF16)
    Bb = pr.S("Bb", [128, 16384], BF16)
    C = pr.S("C", [128, 32768], BF16)
    Dd = pr.S("Dd", [128, 16384], BF16)
    E = pr.S("E", [128, 10240], BF16)
    WOT = pr.S("WOT", [128, 8192], BF16)
    SROW = pr.S("SROW", [128, 1024])
    cst = pr.S("cst", [128, 256 + NPAIR * 128 + 128])
    IDF, RT0 = 0, 256
    cstb = pr.S("cstb", [128, 256], BF16)
    IDB, ONB = 0, 128
    scal = pr.S("scal", [128, 16])
    (C_EXPB, C_EXPS, C_NEG1, C_LN127, C_MAGIC, C_NMAGIC, C_TINY, C_M1R,
     C_SW1, C_M2R, C_SW2, C_NHALF, C_ZERO, C_ONE, C_X1, C_X2) = range(16)
    STAT = pr.S("STAT", [128, 72])
    ST_MEAN, ST_VAR, ST_RSTD, ST_MX, ST_S1 = 12, 13, 14, 16, 17
    ST_W = 32

    Af = A.bitcast(F32)
    Bf = Bb.bitcast(F32)
    Df = Dd.bitcast(F32)
    Ef = E.bitcast(F32)
    WOTf = WOT.bitcast(F32)

    PS0 = pr.P("PS0", [128, 1024])
    PS1 = pr.P("PS1", [128, 1024])
    PS2 = pr.P("PS2", [128, 1024])
    PS3 = pr.P("PS3", [128, 512])
    PSB = pr.P("PSB", [128, 1024], BF16)

    WSTG = Bf
    WQT = Dd
    XTOKB = Dd
    XTB0 = 8192
    WVAT0, WVBT0 = 10240, 12288
    SM_KA, SM_KB = 0, 2048
    KLB0 = 4096
    VLT0 = 0
    XSTG = Af
    XQT0 = 4096
    QSTG0 = 8192 + 1024

    dma = lambda pairs: ("dma", pairs)
    dma_slow = lambda pairs: ("dma", pairs, True)

    st_cst = sch.add("sync", "dw", dma([
        (cst[:, IDF:IDF + 128], d_identf),
        (cst[:, RT0:RT0 + NPAIR * 128], d_rt),
        (cstb[:, IDB:IDB + 128], d_identb),
        (cstb[:, ONB:ONB + 128], d_onesb)]))

    def memset_consts(eng):
        last = None
        for idx, val in [(C_EXPB, -EXP_SHIFT), (C_EXPS, 0.125), (C_NEG1, -1.0),
                         (C_LN127, LN127), (C_MAGIC, MAGIC), (C_NMAGIC, -MAGIC),
                         (C_TINY, 1e-5), (C_NHALF, -0.5), (C_ZERO, 0.0),
                         (C_ONE, 1.0), (C_X1, float(np.log(DH * D)))]:
            last = eng.memset(scal[:, idx:idx + 1], val)
        return last
    st_scal = sch.add("vector", "vv", memset_consts)

    def fold(dram, ntiles, row_free, rhs_sel, out_sel, waits):
        stg = sch.add("sync", "dw", dma([
            (WSTG[:, 0:ntiles * row_free],
             bass.AP(dram.tensor, 0,
                     [[row_free, 128], [128 * row_free, ntiles], [1, row_free]]))]),
            waits=waits)
        evs, mms = [], []
        ncc = row_free // 128

        def fold_bank(k):
            t_, o_ = (PS0, 0) if k % 4 == 0 else (PS0, 512) if k % 4 == 1 else                      (PS1, 0) if k % 4 == 2 else (PS1, 512)
            return t_[:, o_:o_ + 128]
        k = 0
        for ti in range(ntiles):
            for cc in range(ncc):
                w = [stg, st_cst] + ([evs[k - 4]] if k >= 4 else [])

                def mkm(ti=ti, cc=cc, k=k):
                    def fn(eng):
                        return eng.matmul(
                            fold_bank(k),
                            WSTG[:, ti * row_free + cc * 128:
                                 ti * row_free + (cc + 1) * 128],
                            rhs_sel(ti, cc), start=True, stop=True)
                    return fn
                mm = sch.add("tensor", "tt", mkm(), waits=w)
                mms.append(mm)

                def mke(ti=ti, cc=cc, k=k):
                    def fn(eng):
                        dst, off = out_sel(ti, cc)
                        return eng.tensor_copy(dst[:, off:off + 128], fold_bank(k))
                    return fn
                evs.append(sch.add("vector", "vv", mke(), waits=[mm]))
                k += 1
        return stg, mms, evs

    ident_rhs = lambda ti, cc: cst[:, IDF:IDF + 128]
    rope_rhs = lambda ti, cc: cst[:, RT0 + ti * 128:RT0 + (ti + 1) * 128]

    stg_q, mm_q, ev_q = fold(d_wq, 8, 1024, rope_rhs,
                             lambda ti, cc: (WQT, cc * 1024 + ti * 128),
                             waits=[st_cst])

    xload, xcast, xtr, xev = [], [], [], []
    for t in range(SKT):
        buf = t % 2
        w = ([xcast[t - 2]] if t >= 2 else [ev_q[-1]])
        stl = sch.add("sync", "dx",
                      dma([(XSTG[:, buf * 1024:(buf + 1) * 1024],
                            d_xf[t * 128:(t + 1) * 128, :])]), waits=w)
        xload.append(stl)

        def mkc(buf=buf):
            def fn(eng):
                return eng.tensor_copy(Dd[:, XTB0 + buf * 1024:XTB0 + (buf + 1) * 1024],
                                       XSTG[:, buf * 1024:(buf + 1) * 1024])
            return fn
        xcast.append(sch.add("vector", "vc", mkc(), waits=[stl]))
        w2 = [xcast[t]] + ([xev[t - 1]] if t >= 1 else [])

        def mkt(buf=buf):
            def fn(eng):
                last = None
                for dc in range(8):
                    last = eng.transpose(
                        PSB[:, dc * 128:(dc + 1) * 128],
                        Dd[:, XTB0 + buf * 1024 + dc * 128:XTB0 + buf * 1024 + (dc + 1) * 128],
                        cstb[:, IDB:IDB + 128])
                return last
            return fn
        xtr.append(sch.add("tensor", "tt", mkt(), waits=w2))

        def mke(t=t):
            def fn(eng):
                last = None
                for dc in range(8):
                    last = eng.tensor_copy(
                        C[:, dc * S + t * 128:dc * S + (t + 1) * 128],
                        PSB[:, dc * 128:(dc + 1) * 128])
                return last
            return fn
        xev.append(sch.add("vector", "vv", mke(), waits=[xtr[t]]))

    halves = [(0, 9), (9, 8)]
    qspills = []
    prev_half_done = None
    for hi, (t0, nt) in enumerate(halves):
        hw = nt * 128
        hx_tr, hx_ev = [], []
        for j in range(nt):
            ti = t0 + j
            buf = j % 2
            wl = [hx_ev[j - 2]] if j >= 2 else \
                ([prev_half_done] if prev_half_done else [xcast[SKT - 1], xcast[SKT - 2]])
            stl = sch.add("sync", "dx",
                          dma([(XSTG[:, buf * 1024:(buf + 1) * 1024],
                                d_xq[ti * 128:(ti + 1) * 128, :])]), waits=wl)

            def mkc(buf=buf):
                def fn(eng):
                    return eng.tensor_copy(
                        Dd[:, XTB0 + buf * 1024:XTB0 + (buf + 1) * 1024],
                        XSTG[:, buf * 1024:(buf + 1) * 1024])
                return fn
            cp = sch.add("vector", "vc", mkc(),
                         waits=[stl] + ([hx_tr[j - 1]] if j >= 1 else
                                        ([xtr[SKT - 1]] if hi == 0 else [])))

            def mkt(buf=buf):
                def fn(eng):
                    last = None
                    for dc in range(8):
                        last = eng.transpose(
                            PSB[:, dc * 128:(dc + 1) * 128],
                            Dd[:, XTB0 + buf * 1024 + dc * 128:
                               XTB0 + buf * 1024 + (dc + 1) * 128],
                            cstb[:, IDB:IDB + 128])
                    return last
                return fn
            wtr = [cp] + ([hx_ev[j - 1]] if j >= 1 else
                          ([xev[SKT - 1]] if hi == 0 else [prev_half_done]))
            tr = sch.add("tensor", "tt", mkt(), waits=wtr)
            hx_tr.append(tr)

            def mke(j=j, hw=hw):
                def fn(eng):
                    last = None
                    for dc in range(8):
                        last = eng.tensor_copy(
                            A[:, XQT0 + dc * hw + j * 128:XQT0 + dc * hw + (j + 1) * 128],
                            PSB[:, dc * 128:(dc + 1) * 128])
                    return last
                return fn
            hx_ev.append(sch.add("vector", "vv", mke(), waits=[tr]))

        blocks = []
        q0 = 0
        while q0 < hw:
            blocks.append((q0, min(512, hw - q0)))
            q0 += 512
        hq_ev = []
        for p in range(NPAIR):
            for bi, (q0, qw) in enumerate(blocks):
                k = p * len(blocks) + bi
                w = [hx_ev[-1], ev_q[-1]] + ([hq_ev[k - 2]] if k >= 2 else [])

                def mkm(p=p, q0=q0, qw=qw, k=k, hw=hw):
                    def fn(eng):
                        ps = (PS0 if k % 2 == 0 else PS1)[:, 0:qw]
                        last = None
                        for dc in range(8):
                            last = eng.matmul(
                                ps, WQT[:, dc * 1024 + p * 128:dc * 1024 + (p + 1) * 128],
                                A[:, XQT0 + dc * hw + q0:XQT0 + dc * hw + q0 + qw],
                                start=(dc == 0), stop=(dc == 7))
                        return last
                    return fn
                mm = sch.add("tensor", "tt", mkm(), waits=w)

                def mke(k=k, qw=qw):
                    def fn(eng):
                        return eng.tensor_copy(E[:, QSTG0 + (k % 2) * 512:
                                                 QSTG0 + (k % 2) * 512 + qw],
                                               (PS0 if k % 2 == 0 else PS1)[:, 0:qw])
                    return fn
                ev = sch.add("vector", "vv", mke(), waits=[mm] +
                             ([qspills[-2]] if len(qspills) >= 2 else []))
                hq_ev.append(ev)

                qspills.append(sch.add("sync", "do", dma(
                    [(d_qsp[p, :, t0 * 128 + q0:t0 * 128 + q0 + qw],
                      E[:, QSTG0 + (k % 2) * 512:QSTG0 + (k % 2) * 512 + qw])]),
                    waits=[ev]))
        prev_half_done = hq_ev[-1]
    q_done = [qspills[-1], qspills[-2]]

    stg_ka, mm_ka, ev_ka = fold(d_wka, 2, 1024, ident_rhs,
                                lambda ti, cc: (E, SM_KA + cc * 256 + ti * 128),
                                waits=[prev_half_done])
    stg_kb, mm_kb, ev_kb = fold(d_wkb, 8, 256, rope_rhs,
                                lambda ti, cc: (E, SM_KB + cc * 1024 + ti * 128),
                                waits=[mm_ka[-1]])
    stg_va, mm_va, ev_va = fold(d_wva, 2, 1024, ident_rhs,
                                lambda ti, cc: (Dd, WVAT0 + cc * 256 + ti * 128),
                                waits=[mm_kb[-1], prev_half_done])
    stg_vb, mm_vb, ev_vb = fold(d_wvb, 8, 256, ident_rhs,
                                lambda ti, cc: (Dd, WVBT0 + cc * 1024 + ti * 128),
                                waits=[mm_va[-1]])
    stg_wo, mm_wo, ev_wo = fold(d_wo, 8, 1024, ident_rhs,
                                lambda ti, cc: (WOT, cc * 1024 + ti * 128),
                                waits=[mm_vb[-1]])

    def KTs(p):
        return (A, p * S) if p < 4 else (Bb, (p - 4) * S)

    kev = []
    prev_kl = None
    for sb in range(8):
        w = [ev_ka[-1], xev[-1], q_done[0], ev_wo[-1]] + ([prev_kl] if prev_kl else [])

        def mkkl(sb=sb):
            def fn(eng):
                last = None
                for lc in range(2):
                    for dc in range(8):
                        last = eng.matmul(
                            PS2[:, lc * 512:(lc + 1) * 512],
                            E[:, SM_KA + dc * 256 + lc * 128:SM_KA + dc * 256 + (lc + 1) * 128],
                            C[:, dc * S + sb * 512:dc * S + (sb + 1) * 512],
                            start=(dc == 0), stop=(dc == 7))
                return last
            return fn
        klmm = sch.add("tensor", "tt", mkkl(), waits=w)

        def mkkle(eng):
            return eng.tensor_copy(E[:, KLB0:KLB0 + 1024], PS2[:, 0:1024])
        klev = sch.add("vector", "vv", mkkle, waits=[klmm])

        kmm_l = None
        for p in range(NPAIR):
            w2 = [klev, ev_kb[-1]] + ([kev[-2]] if len(kev) >= 2 else [])

            def mkk(p=p, sb=sb):
                def fn(eng):
                    ps = (PS0 if p % 2 == 0 else PS1)[:, 0:512]
                    last = None
                    for lc in range(2):
                        last = eng.matmul(
                            ps, E[:, SM_KB + lc * 1024 + p * 128:
                                  SM_KB + lc * 1024 + (p + 1) * 128],
                            E[:, KLB0 + lc * 512:KLB0 + (lc + 1) * 512],
                            start=(lc == 0), stop=(lc == 1))
                    return last
                return fn
            kmm = sch.add("tensor", "tt", mkk(), waits=w2)
            kmm_l = kmm

            def mkke(p=p, sb=sb):
                def fn(eng):
                    dst, off = KTs(p)
                    return eng.tensor_copy(dst[:, off + sb * 512:off + (sb + 1) * 512],
                                           (PS0 if p % 2 == 0 else PS1)[:, 0:512])
                return fn
            kev.append(sch.add("vector", "vv", mkke(), waits=[kmm]))
        prev_kl = kmm_l

    vlev = []
    vlk = 0
    for lc in range(2):
        for sb in range(8):
            w = [ev_va[-1], kev[-1]] if vlk < 2 else [vlev[vlk - 2]]

            def mkvl(lc=lc, sb=sb, vlk=vlk):
                def fn(eng):
                    ps = (PS0 if vlk % 2 == 0 else PS1)[:, 0:512]
                    last = None
                    for dc in range(8):
                        last = eng.matmul(
                            ps, Dd[:, WVAT0 + dc * 256 + lc * 128:
                                   WVAT0 + dc * 256 + (lc + 1) * 128],
                            C[:, dc * S + sb * 512:dc * S + (sb + 1) * 512],
                            start=(dc == 0), stop=(dc == 7))
                    return last
                return fn
            mm = sch.add("tensor", "tt", mkvl(), waits=w)

            def mkvle(lc=lc, sb=sb, vlk=vlk):
                def fn(eng):
                    return eng.tensor_copy(
                        E[:, VLT0 + lc * S + sb * 512:VLT0 + lc * S + (sb + 1) * 512],
                        (PS0 if vlk % 2 == 0 else PS1)[:, 0:512])
                return fn
            vlev.append(sch.add("vector", "vv", mkvle(), waits=[mm]))
            vlk += 1

    vev = []
    for ti in range(SKT):
        w = [vlev[-1], ev_vb[-1]] + ([vev[ti - 2]] if ti >= 2 else [])

        def mkv(ti=ti):
            def fn(eng):
                ps = PS0 if ti % 2 == 0 else PS1
                last = None
                for half in range(2):
                    for lc in range(2):
                        last = eng.matmul(
                            ps[:, half * 512:(half + 1) * 512],
                            E[:, VLT0 + lc * S + ti * 128:VLT0 + lc * S + (ti + 1) * 128],
                            Dd[:, WVBT0 + lc * 1024 + half * 512:
                               WVBT0 + lc * 1024 + (half + 1) * 512],
                            start=(lc == 0), stop=(lc == 1))
                return last
            return fn
        mm = sch.add("tensor", "tt", mkv(), waits=w)

        def mkve(ti=ti):
            def fn(eng):
                return eng.tensor_copy(C[:, ti * 1024:(ti + 1) * 1024],
                                       (PS0 if ti % 2 == 0 else PS1)[:, :])
            return fn
        vev.append(sch.add("vector", "vv", mkve(), waits=[mm]))

    p0_done = [vev[-1], ev_wo[-1], kev[-1], st_scal]

    if "kchk" in taps:
        sch.add("gpsimd", "do", dma([(taps["kchk"], A[:, 0:S])]), waits=p0_done)
    if "qchk" in taps:
        sch.add("gpsimd", "do", dma([(taps["qchk"], d_qsp[0, :, :])]),
                waits=[qspills[-1]])
    if "vchk" in taps:
        sch.add("gpsimd", "do", dma([(taps["vchk"], C[:, 0:1024])]), waits=p0_done)

    EXPB0, USB0, QTQ0, G1T0, G1S0, MXR0 = 0, 2048, 6144, 10240, 11264, 12288
    XRS0, YW0, X2W0, RDEN0, LNB0 = 0, 1024, 2048, 3072, 3584

    resload = [None] * QTN
    lnst = {}
    uev = []
    rden_step = None
    prev_wo_evac = []
    qtq_loads = []
    last_sc_of_qb = []
    prev_exp, prev_av = None, None

    pa_items = ([("w1", t, s, t) for t in range(32) for s in range(2)] +
                [("w2", t, s, 32 + t) for t in range(8) for s in range(8)])
    pa_reds, pa_steps, pa_fin = [], [], {}

    for qbi, (q0, qw) in enumerate(QBLOCKS):
        wq_ = list(p0_done) if qbi == 0 else [last_sc_of_qb[-1]]
        qtq = sch.add("sync", "dq", dma(
            [(Dd[:, QTQ0 + p * 512:QTQ0 + p * 512 + qw], d_qsp[p, :, q0:q0 + qw])
             for p in range(NPAIR)]), waits=wq_)
        qtq_loads.append(qtq)

        pair_u = []
        for p in range(NPAIR):
            sc_steps, exp_steps, av_steps = [], [], []
            pend_av = None
            for kt in range(SKT):
                w = []
                if kt == 0:
                    w.append(qtq)
                    if prev_exp is not None:
                        w.append(prev_exp[SKT - 2])
                    else:
                        w += p0_done
                if kt == 1 and prev_exp is not None:
                    w.append(prev_exp[SKT - 1])
                if kt >= 2:
                    w.append(exp_steps[kt - 2])

                def mksc(p=p, qw=qw, kt=kt):
                    def fn(eng):
                        ps = PS0 if kt % 2 == 0 else PS1
                        ka, koff = KTs(p)
                        eng.matmul(ps[:, 0:qw],
                                   ka[0:64, koff + kt * 128:koff + (kt + 1) * 128],
                                   Dd[0:64, QTQ0 + p * 512:QTQ0 + p * 512 + qw],
                                   start=True, stop=True, tile_position=(0, 0))
                        return eng.matmul(
                            ps[:, 512:512 + qw],
                            ka[64:128, koff + kt * 128:koff + (kt + 1) * 128],
                            Dd[64:128, QTQ0 + p * 512:QTQ0 + p * 512 + qw],
                            start=True, stop=True, tile_position=(64, 0))
                    return fn
                sc = sch.add("tensor", "te", mksc(), waits=w)
                sc_steps.append(sc)

                we = [sc]
                if kt >= 2:
                    we.append(av_steps[kt - 2])
                elif prev_av is not None:
                    we.append(prev_av[SKT - 2 + kt])

                def mkexp(qw=qw, kt=kt):
                    def fn(eng):
                        ps = PS0 if kt % 2 == 0 else PS1
                        eb0 = EXPB0 + (kt % 2) * 1024
                        if qw == 512:
                            return eng.activation(
                                Dd[:, eb0:eb0 + 1024], ps[:, 0:1024], ACT.Exp,
                                bias=scal[:, C_EXPB:C_EXPB + 1],
                                scale=scal[:, C_EXPS:C_EXPS + 1])
                        eng.activation(Dd[:, eb0:eb0 + qw], ps[:, 0:qw], ACT.Exp,
                                       bias=scal[:, C_EXPB:C_EXPB + 1],
                                       scale=scal[:, C_EXPS:C_EXPS + 1])
                        return eng.activation(
                            Dd[:, eb0 + 512:eb0 + 512 + qw], ps[:, 512:512 + qw],
                            ACT.Exp, bias=scal[:, C_EXPB:C_EXPB + 1],
                            scale=scal[:, C_EXPS:C_EXPS + 1])
                    return fn
                ex = sch.add("scalar", "se", mkexp(), waits=we)
                exp_steps.append(ex)

                if pend_av is not None:
                    av_steps.append(sch.add("tensor", "te", pend_av[0],
                                            waits=pend_av[1]))

                def mkav(p=p, qw=qw, kt=kt):
                    def fn(eng):
                        eb0 = EXPB0 + (kt % 2) * 1024
                        st = (kt == 0); sp = (kt == SKT - 1)
                        eng.matmul(PS2[0:64, 0:qw],
                                   C[:, kt * 1024 + p * 128:kt * 1024 + p * 128 + 64],
                                   Dd[:, eb0:eb0 + qw], start=st, stop=sp,
                                   tile_position=(0, 0))
                        eng.matmul(PS2[64:128, 0:qw],
                                   C[:, kt * 1024 + p * 128 + 64:kt * 1024 + (p + 1) * 128],
                                   Dd[:, eb0 + 512:eb0 + 512 + qw], start=st, stop=sp,
                                   tile_position=(0, 64))
                        eng.matmul(PS2[0:64, 512:512 + qw], cstb[0:128, ONB:ONB + 64],
                                   Dd[:, eb0:eb0 + qw], start=st, stop=sp,
                                   tile_position=(0, 0))
                        return eng.matmul(PS2[64:128, 512:512 + qw],
                                          cstb[0:128, ONB:ONB + 64],
                                          Dd[:, eb0 + 512:eb0 + 512 + qw],
                                          start=st, stop=sp, tile_position=(0, 64))
                    return fn
                wa = [ex]
                if kt == 0:
                    if uev:
                        wa += [uev[-1], rden_step]
                    else:
                        wa += p0_done
                    wa += prev_wo_evac
                pend_av = (mkav(), wa)
            av_steps.append(sch.add("tensor", "te", pend_av[0],
                                    waits=pend_av[1]))
            prev_exp, prev_av = exp_steps, av_steps

            def mkrd(qw=qw):
                def fn(eng):
                    eng.activation(Ef[:, LNB0:LNB0 + qw], PS2[:, 512:512 + qw], ACT.Ln)
                    return eng.activation(Ef[:, RDEN0:RDEN0 + qw],
                                          Ef[:, LNB0:LNB0 + qw], ACT.Exp,
                                          bias=scal[:, C_ZERO:C_ZERO + 1],
                                          scale=scal[:, C_NEG1:C_NEG1 + 1])
                return fn
            rden_step = sch.add("scalar", "ss", mkrd(), waits=[av_steps[-1]])

            def mku(p=p, qw=qw):
                def fn(eng):
                    return eng.tensor_mul(Dd[:, USB0 + p * 512:USB0 + p * 512 + qw],
                                          PS2[:, 0:qw], Ef[:, RDEN0:RDEN0 + qw])
                return fn
            ue = sch.add("vector", "ve", mku(), waits=[rden_step])
            uev.append(ue)
            pair_u.append(ue)
            if "usb" in taps and qbi == 0 and p == 0:
                sch.add("gpsimd", "do",
                        dma([(taps["usb"], Dd[:, USB0:USB0 + 512])]), waits=[ue])
        last_sc_of_qb.append(sc_steps[-1])

        if "den" in taps and qbi == 0:
            sch.add("sync", "do", dma([(taps["den"], Ef[:, RDEN0:RDEN0 + 512])]),
                    waits=[pair_u[-1]])

        ntile = qw // 128
        for sub in range(ntile):
            ti = q0 // 128 + sub
            xb = XRS0 + (ti % 2) * 4096
            wr = list(p0_done) if ti < 2 else [lnst[ti - 2]["ydone"]]
            resload[ti] = sch.add("sync", "dr", dma(
                [(Ef[:, xb:xb + 1024], d_xq[ti * 128:(ti + 1) * 128, :])]),
                waits=wr)

            w = list(pair_u) if sub == 0 else [lnst[ti - 1]["ydone"]]

            def mkwo(sub=sub):
                def fn(eng):
                    last = None
                    for half in range(2):
                        for p in range(NPAIR):
                            last = eng.matmul(
                                PS2[:, half * 512:(half + 1) * 512],
                                Dd[:, USB0 + p * 512 + sub * 128:USB0 + p * 512 + (sub + 1) * 128],
                                WOT[:, p * 1024 + half * 512:p * 1024 + (half + 1) * 512],
                                start=(p == 0), stop=(p == NPAIR - 1))
                    return last
                return fn
            wo = sch.add("tensor", "tt", mkwo(), waits=w)

            def mky(xb=xb):
                def fn(eng):
                    eng.tensor_add(Ef[:, YW0:YW0 + 512], PS2[:, 0:512],
                                   Ef[:, xb:xb + 512])
                    return eng.tensor_add(Ef[:, YW0 + 512:YW0 + 1024],
                                          PS2[:, 512:1024],
                                          Ef[:, xb + 512:xb + 1024])
                return fn
            ydone = sch.add("vector", "vv", mky(), waits=[wo, resload[ti]])
            ywtap = None
            if "yw" in taps and ti == 0:
                ywtap = sch.add("sync", "do",
                                dma([(taps["yw"], Ef[:, YW0:YW0 + 1024])]),
                                waits=[ydone])

            bn_a = sch.add("vector", "vv",
                           lambda eng: eng.reduce_sum(STAT[:, 20:21],
                                                      Ef[:, YW0:YW0 + 1024], AX),
                           waits=[ydone])
            bn_b = sch.add("vector", "vv",
                           lambda eng: eng.tensor_single_scalar(
                               STAT[:, ST_MEAN:ST_MEAN + 1], STAT[:, 20:21],
                               1.0 / D, ALU.mult), waits=[bn_a])

            def mkbn_c(eng):
                eng.tensor_scalar(Ef[:, X2W0:X2W0 + 1024], Ef[:, YW0:YW0 + 1024],
                                  STAT[:, ST_MEAN:ST_MEAN + 1], 0.0,
                                  ALU.subtract, ALU.add)
                eng.tensor_mul(Ef[:, YW0:YW0 + 1024], Ef[:, X2W0:X2W0 + 1024],
                               Ef[:, X2W0:X2W0 + 1024])
                return eng.reduce_sum(STAT[:, 21:22], Ef[:, YW0:YW0 + 1024], AX)
            bn_c = sch.add("vector", "vv", mkbn_c,
                           waits=[bn_b] + ([ywtap] if ywtap else []))
            bn = sch.add("vector", "vv",
                         lambda eng: eng.tensor_single_scalar(
                             STAT[:, ST_VAR:ST_VAR + 1], STAT[:, 21:22],
                             1.0 / D, ALU.mult), waits=[bn_c])

            def mkrstd(eng):
                eng.activation(STAT[:, ST_RSTD:ST_RSTD + 1],
                               STAT[:, ST_VAR:ST_VAR + 1], ACT.Ln,
                               bias=scal[:, C_TINY:C_TINY + 1],
                               scale=scal[:, C_ONE:C_ONE + 1])
                return eng.activation(STAT[:, ST_RSTD:ST_RSTD + 1],
                                      STAT[:, ST_RSTD:ST_RSTD + 1], ACT.Exp,
                                      bias=scal[:, C_ZERO:C_ZERO + 1],
                                      scale=scal[:, C_NHALF:C_NHALF + 1])
            sttap = None
            if "stt" in taps and ti == 0:
                sttap = sch.add("sync", "do", dma([(taps["stt"], STAT[:, :])]),
                                waits=[bn])
                sch.add("sync", "do", dma([(taps["x2w"], Ef[:, X2W0:X2W0 + 1024])]),
                        waits=[bn]) if "x2w" in taps else None
            rstd = sch.add("scalar", "s2", mkrstd,
                           waits=[bn] + ([sttap] if sttap else []))

            def mkx2(eng):
                eng.tensor_scalar(Ef[:, X2W0:X2W0 + 1024], Ef[:, X2W0:X2W0 + 1024],
                                  STAT[:, ST_RSTD:ST_RSTD + 1], 0.0,
                                  ALU.mult, ALU.add)
                return eng.reduce_max(STAT[:, 20:21], Ef[:, X2W0:X2W0 + 1024],
                                      AX, apply_absolute_value=True)
            x2done = sch.add("vector", "vv", mkx2,
                             waits=[rstd] + ([lnst[ti - 1]["spx2"]] if ti >= 1 else []))
            x2done = sch.add("vector", "vv",
                             lambda eng: eng.tensor_single_scalar(
                                 STAT[:, ST_MX:ST_MX + 1], STAT[:, 20:21],
                                 1e-5, ALU.max), waits=[x2done])

            def mks1(eng):
                eng.activation(STAT[:, ST_S1:ST_S1 + 1], STAT[:, ST_MX:ST_MX + 1],
                               ACT.Ln)
                return eng.activation(STAT[:, ST_S1:ST_S1 + 1],
                                      STAT[:, ST_S1:ST_S1 + 1], ACT.Exp,
                                      bias=scal[:, C_LN127:C_LN127 + 1],
                                      scale=scal[:, C_NEG1:C_NEG1 + 1])
            s1 = sch.add("scalar", "s2", mks1, waits=[x2done])

            def mkg1b(eng):
                eng.tensor_scalar(Ef[:, YW0:YW0 + 1024], Ef[:, X2W0:X2W0 + 1024],
                                  STAT[:, ST_S1:ST_S1 + 1], MAGIC, ALU.mult, ALU.add)
                return eng.tensor_single_scalar(Dd[:, G1T0:G1T0 + 1024],
                                                Ef[:, YW0:YW0 + 1024], MAGIC,
                                                ALU.subtract)
            g1done = sch.add("vector", "vv", mkg1b,
                             waits=[s1] + ([ywtap] if ywtap else []))

            spx2 = sch.add("sync", "do", dma(
                [(d_x2sp[ti * 128:(ti + 1) * 128, :], Ef[:, X2W0:X2W0 + 1024])]),
                waits=[x2done])

            wg = [g1done] + ([lnst[ti - 1]["g1ev"]] if ti >= 1 else [])

            def mkg1t(eng):
                last = None
                for dc in range(8):
                    last = eng.transpose(PSB[:, dc * 128:(dc + 1) * 128],
                                         Dd[:, G1T0 + dc * 128:G1T0 + (dc + 1) * 128],
                                         cstb[:, IDB:IDB + 128])
                return last
            g1t = sch.add("tensor", "tt", mkg1t, waits=wg)

            def mkmxr(eng):
                return eng.matmul(PS3[0:1, 0:128], STAT[:, ST_MX:ST_MX + 1],
                                  cst[:, IDF:IDF + 128], start=True, stop=True)
            mxr = sch.add("tensor", "tt", mkmxr,
                          waits=[x2done] + ([lnst[ti - 1]["mxev"]] if ti >= 1 else []))

            def mkg1e(eng):
                return eng.tensor_copy(Dd[:, G1S0:G1S0 + 1024], PSB[:, 0:1024])
            g1ev = sch.add("vector", "vv", mkg1e,
                           waits=[g1t] + ([lnst[ti - 1]["spg1"]] if ti >= 1 else []))

            def mkmxe(ti=ti):
                def fn(eng):
                    return eng.tensor_copy(Dd[0:1, MXR0 + ti * 128:MXR0 + (ti + 1) * 128],
                                           PS3[0:1, 0:128])
                return fn
            mxev = sch.add("vector", "vv", mkmxe(), waits=[mxr])

            spg1 = sch.add("sync", "do", dma(
                [(d_g1sp[dc, :, ti * 128:(ti + 1) * 128],
                  Dd[:, G1S0 + dc * 128:G1S0 + (dc + 1) * 128]) for dc in range(8)]),
                waits=[g1ev], n_inc=1)

            lnst[ti] = {"ydone": ydone, "x2done": x2done, "g1ev": g1ev,
                        "mxev": mxev, "spx2": spx2, "spg1": spg1}
            prev_wo_evac = [ydone]

            for it in range(ti * 8, min((ti + 1) * 8, len(pa_items))):
                nm_, t_, s_, col_ = pa_items[it]
                dram_ = d_w1 if nm_ == "w1" else d_w2
                slot = (it % 2) * 512
                wld = [pa_reds[it - 2]] if it >= 2 else list(p0_done)
                ld_ = sch.add("gpsimd", "dq", dma(
                    [(SROW[:, slot:slot + 512],
                      dram_[t_ * 128:(t_ + 1) * 128,
                            s_ * 512:(s_ + 1) * 512])]), waits=wld)

                def mkpar(s_=s_, col_=col_, slot=slot):
                    def fn(eng):
                        dst = (STAT[:, ST_W + col_:ST_W + col_ + 1] if s_ == 0
                               else STAT[:, 18 + slot // 512:19 + slot // 512])
                        return eng.reduce_sum(dst, SROW[:, slot:slot + 512], AX,
                                              apply_absolute_value=True)
                    return fn
                red_ = sch.add("vector", "vv", mkpar(), waits=[ld_])
                pa_reds.append(red_)
                last_ = red_
                if s_ != 0:
                    def mkpaa(col_=col_, slot=slot):
                        def fn(eng):
                            return eng.tensor_add(
                                STAT[:, ST_W + col_:ST_W + col_ + 1],
                                STAT[:, ST_W + col_:ST_W + col_ + 1],
                                STAT[:, 18 + slot // 512:19 + slot // 512])
                        return fn
                    last_ = sch.add("vector", "vv", mkpaa(), waits=[red_])
                pa_steps.append(last_)
                if it == 63:
                    pa_fin["w1"] = last_
                if it == 127:
                    pa_fin["w2"] = last_

    def mkmxcast(eng):
        eng.tensor_copy(SROW[0:1, 0:1024], Dd[0:1, MXR0:MXR0 + 1024])
        eng.tensor_copy(SROW[32:33, 0:1024], Dd[0:1, MXR0 + 1024:MXR0 + 2048])
        return eng.tensor_copy(SROW[64:65, 0:LQ - 2048],
                               Dd[0:1, MXR0 + 2048:MXR0 + LQ])
    mxcast = sch.add("vector", "vv", mkmxcast,
                     waits=[lnst[QTN - 1]["mxev"]])
    spmx = sch.add("sync", "do", dma(
        [(d_mxsp[0:1, 0:1024], SROW[0:1, 0:1024]),
         (d_mxsp[0:1, 1024:2048], SROW[32:33, 0:1024]),
         (d_mxsp[0:1, 2048:LQ], SROW[64:65, 0:LQ - 2048])]), waits=[mxcast])
    if "mx" in taps:
        sch.add("sync", "do", dma(
            [(taps["mx"][0:1, 0:1024], SROW[0:1, 0:1024]),
             (taps["mx"][0:1, 1024:2048], SROW[32:33, 0:1024]),
             (taps["mx"][0:1, 2048:LQ], SROW[64:65, 0:LQ - 2048])]), waits=[mxcast])

    if "x2" in taps:
        sch.add("sync", "do", dma([(taps["x2"], d_x2sp)]),
                waits=[lnst[ti]["spx2"] for ti in range(QTN)])
    if "g1" in taps:
        sch.add("gpsimd", "do", dma([(taps["g1"], d_g1sp)]),
                waits=[lnst[ti]["spg1"] for ti in range(QTN)])
    p1_done = [lnst[QTN - 1]["g1ev"], mxcast, uev[-1]]
    p1_sp = [lnst[ti]["spx2"] for ti in range(QTN)] + \
            [lnst[ti]["spg1"] for ti in range(QTN)] + [spmx]

    WSTG2 = Df
    CWB0 = 256 + NPAIR * 128
    st_cw = sch.add("sync", "dw", dma_slow(
        [(cst[:, CWB0:CWB0 + 32],
          bass.AP(d_cw.tensor, 0, [[3, 128], [384, 32], [1, 1]])),
         (cst[:, CWB0 + 32:CWB0 + 64], bass.AP(d_cw.tensor, 1, [[3, 128], [384, 32], [1, 1]])),
         (cst[:, CWB0 + 64:CWB0 + 96], bass.AP(d_cw.tensor, 2, [[3, 128], [384, 32], [1, 1]])),
         (cst[:, CWB0 + 96:CWB0 + 128], bass.AP(d_cb.tensor, 0, [[1, 128], [128, 32]]))]),
        waits=p1_done)
    CW0, CW1, CW2, CB0 = CWB0, CWB0 + 32, CWB0 + 64, CWB0 + 96

    def wquant(dram, ntiles, row_free, outT_sel, c_sw, c_mr, waits0,
               stat_ofs, rs_col, pa_last):
        nsub = row_free // 1024

        def mkrs(eng):
            return eng.reduce_sum(STAT[:, rs_col:rs_col + 1],
                                  STAT[:, ST_W + stat_ofs:
                                       ST_W + stat_ofs + ntiles], AX)
        rs = sch.add("vector", "vq", mkrs, waits=[pa_last])
        tot = sch.add("tensor", "tq",
                      lambda eng: eng.matmul(PS3[0:1, 0:1],
                                             STAT[:, rs_col:rs_col + 1],
                                             scal[:, C_ONE:C_ONE + 1],
                                             start=True, stop=True),
                      waits=[rs, st_scal, lnst[QTN - 1]["mxev"], spmx])

        def mksw(eng):
            eng.activation(SROW[0:1, 641:642], PS3[0:1, 0:1], ACT.Ln)
            return eng.activation(SROW[0:1, 642:643], SROW[0:1, 641:642], ACT.Exp,
                                  bias=scal[0:1, C_X1:C_X1 + 1],
                                  scale=scal[0:1, C_NEG1:C_NEG1 + 1])
        sw = sch.add("scalar", "sq", mksw, waits=[tot])

        def mkhl_a(eng):
            eng.tensor_single_scalar(SROW[0:1, 645:646], SROW[0:1, 642:643],
                                     1e5, ALU.min)
            return eng.tensor_copy(SROW[0:1, 640:641], PS3[0:1, 0:1])
        hl_a = sch.add("vector", "vq", mkhl_a, waits=[sw])

        def mkhl_b(eng):
            eng.tensor_copy(E[0:1, 0:1], SROW[0:1, 645:646])
            return eng.tensor_copy(E[0:1, 2:3], SROW[0:1, 640:641])
        hl_b = sch.add("vector", "vq", mkhl_b, waits=[hl_a])

        def mkhl_c(eng):
            eng.tensor_sub(SROW[0:1, 643:644], SROW[0:1, 645:646], E[0:1, 0:1])
            return eng.tensor_sub(SROW[0:1, 644:645], SROW[0:1, 640:641],
                                  E[0:1, 2:3])
        hl_c = sch.add("vector", "vq", mkhl_c, waits=[hl_b])

        def mkhl_d(eng):
            eng.tensor_copy(E[0:1, 1:2], SROW[0:1, 643:644])
            return eng.tensor_copy(E[0:1, 3:4], SROW[0:1, 644:645])
        hl = sch.add("vector", "vq", mkhl_d, waits=[hl_c])

        def mkbc(eng):
            eng.matmul(PS3[:, 0:1], cstb[0:1, ONB:ONB + 128], E[0:1, 0:1],
                       start=True, stop=False)
            eng.matmul(PS3[:, 0:1], cstb[0:1, ONB:ONB + 128], E[0:1, 1:2],
                       start=False, stop=True)
            eng.matmul(PS3[:, 1:2], cstb[0:1, ONB:ONB + 128], E[0:1, 2:3],
                       start=True, stop=False)
            return eng.matmul(PS3[:, 1:2], cstb[0:1, ONB:ONB + 128], E[0:1, 3:4],
                              start=False, stop=True)
        bc = sch.add("tensor", "tq", mkbc, waits=[hl])

        def mksc_a(eng):
            eng.tensor_copy(scal[:, c_sw:c_sw + 1], PS3[:, 0:1])
            return eng.tensor_scalar(STAT[:, 22:23], PS3[:, 1:2],
                                     1.0 / (DH * D), 1e-5, ALU.mult, ALU.max)
        sc_a = sch.add("vector", "vq", mksc_a, waits=[bc])
        sc = sch.add("vector", "vq",
                     lambda eng: eng.tensor_single_scalar(
                         scal[:, c_mr:c_mr + 1], STAT[:, 22:23],
                         1.0 / 127.0, ALU.mult), waits=[sc_a])

        tlast = None
        prev_tr_ev = None
        for t in range(ntiles):
            buf = t % 2
            w = ([sc] + list(waits0)) if t < 2 else [tern_steps[t - 2]]
            ld = sch.add("sync", "dq", dma(
                [(WSTG2[:, buf * row_free:(buf + 1) * row_free],
                  dram[t * 128:(t + 1) * 128, :])]), waits=w)

            def mkq(t=t, buf=buf):
                def fn(eng):
                    last = None
                    for ss_ in range(nsub):
                        o = buf * row_free + ss_ * 1024
                        eng.tensor_scalar(SROW[:, 0:1024], WSTG2[:, o:o + 1024],
                                          scal[:, c_sw:c_sw + 1], MAGIC,
                                          ALU.mult, ALU.add)
                        eng.tensor_scalar(SROW[:, 0:1024], SROW[:, 0:1024],
                                          -MAGIC, -1.0, ALU.add, ALU.max)
                        last = eng.tensor_single_scalar(
                            E[:, (ss_ % 8) * 1024:(ss_ % 8 + 1) * 1024]
                            if nsub > 1 else E[:, (t % 8) * 1024:(t % 8 + 1) * 1024],
                            SROW[:, 0:1024], 1.0, ALU.min)
                    return last
                return fn
            qs = sch.add("vector", "vq", mkq(), waits=[ld] +
                         ([prev_tr_ev] if prev_tr_ev else []))

            ngrp = (row_free // 128)
            grp_ev = None
            for g0 in range(0, ngrp, 8):
                def mktr(t=t, g0=g0, nsub=nsub):
                    def fn(eng):
                        last = None
                        for j in range(min(8, ngrp - g0)):
                            cc = g0 + j
                            src_slot = (cc // 8) * 1024 if nsub > 1 else (t % 8) * 1024
                            src_off = (cc % 8) * 128 if nsub > 1 else cc * 128
                            last = eng.transpose(
                                PSB[:, j * 128:(j + 1) * 128],
                                E[:, src_slot + src_off:src_slot + src_off + 128],
                                cstb[:, IDB:IDB + 128])
                        return last
                    return fn
                wtr = [qs] + ([grp_ev] if grp_ev else ([prev_tr_ev] if prev_tr_ev else []))
                tr = sch.add("tensor", "tq", mktr(), waits=wtr)

                def mkev(t=t, g0=g0):
                    def fn(eng):
                        last = None
                        for j in range(min(8, ngrp - g0)):
                            cc = g0 + j
                            dst, off = outT_sel(t, cc)
                            last = eng.tensor_copy(dst[:, off:off + 128],
                                                   PSB[:, j * 128:(j + 1) * 128])
                        return last
                    return fn
                grp_ev = sch.add("vector", "vq", mkev(), waits=[tr])
            prev_tr_ev = grp_ev
            tern_steps.append(grp_ev)
            tlast = grp_ev
        return tlast

    def t1_sel(t, cc):
        dst = A if cc < 4 else Bb
        return dst, (cc % 4) * 4096 + t * 128

    def t2_sel(t, cc):
        return C, cc * 1024 + t * 128

    tern_steps = []
    t1_last = wquant(d_w1, 32, 1024, t1_sel, C_SW1, C_M1R,
                     waits0=list(p1_done) + p1_sp,
                     stat_ofs=0, rs_col=22, pa_last=pa_fin["w1"])
    tern_steps = []
    t2_last = wquant(d_w2, 8, 4096, t2_sel, C_SW2, C_M2R, waits0=[t1_last],
                     stat_ofs=32, rs_col=23, pa_last=pa_fin["w2"])

    p2_done = [t1_last, t2_last, st_cw]
    if "t1" in taps:
        sch.add("gpsimd", "do", dma([(taps["t1"], A[:, 0:4096])]), waits=[t1_last])
    if "t2" in taps:
        sch.add("gpsimd", "do", dma([(taps["t2"], C[:, 0:1024])]), waits=p2_done)

    CONVO = Df
    G1SL0 = 0
    H1S0 = 2064
    H1SB0 = 5700
    T0B0 = 4096
    ABSB0 = 4608
    RRB0 = 2324
    S2HI0, S2LO0 = 2582, 2838
    RBCf0 = 1547
    S2Ef0 = 1805
    YW3f0 = 2061
    MMXf0 = 2317
    X2SL0, OUTW0, Y2W0 = 0, 2048, 3072
    R_ROW0 = 640

    prev_out = None
    prev_h1last = None
    prev_rbce = None
    prev_w2ev = None
    w2evs = []
    prev_s2e = None
    prev_mxuse = None
    out_steps = []
    for ci, (t0, tw) in enumerate(FFN_CHUNKS):
        hw = tw + 2
        g1c0 = t0 - 1
        lo_pad = (t0 == 0)
        hi_pad = (t0 + tw == LQ)
        ld_lo = max(0, g1c0)
        ld_hi = min(LQ, g1c0 + hw)
        ldw = ld_hi - ld_lo
        dst0 = ld_lo - g1c0

        w = list(p2_done) + p1_sp if ci == 0 else [prev_h1last]
        g1ld = sch.add("sync", "dq", dma(
            [(E[:, G1SL0 + dc * hw + dst0:G1SL0 + dc * hw + dst0 + ldw],
              d_g1sp[dc, :, ld_lo:ld_hi]) for dc in range(8)]), waits=w)
        mxld = sch.add("sync", "dq", dma(
            [(SROW[96:97, 0:ldw], d_mxsp[0:1, ld_lo:ld_hi])]),
            waits=([spmx] + list(p2_done)) if ci == 0 else
                  [prev_rbce, prev_mxuse])
        padsteps = [g1ld]
        if lo_pad or hi_pad:
            def mkpad(lo_pad=lo_pad, hi_pad=hi_pad, hw=hw):
                def fn(eng):
                    last = None
                    for dc in range(8):
                        if lo_pad:
                            last = eng.memset(E[:, G1SL0 + dc * hw:G1SL0 + dc * hw + 1], 0.0)
                        if hi_pad:
                            last = eng.memset(
                                E[:, G1SL0 + dc * hw + hw - 1:G1SL0 + dc * hw + hw], 0.0)
                    return last
                return fn
            padsteps.append(sch.add("vector", "vf", mkpad(), waits=[g1ld]))

        x2lds = []
        for sub in range(tw // 128):
            x2lds.append(sch.add("sync", "dr", dma(
                [(WOTf[:, X2SL0 + sub * 1024:X2SL0 + (sub + 1) * 1024],
                  d_x2sp[t0 + sub * 128:t0 + (sub + 1) * 128, :])]),
                waits=[prev_out] if prev_out else list(p2_done)))

        def mkrr_a(hw=hw, dst0=dst0, ldw=ldw):
            def fn(eng):
                eng.memset(SROW[0:1, R_ROW0:R_ROW0 + hw], 0.0)
                return eng.tensor_scalar(
                    SROW[0:1, R_ROW0 + dst0:R_ROW0 + dst0 + ldw],
                    SROW[96:97, 0:ldw], scal[0:1, C_M1R:C_M1R + 1], 0.0,
                    ALU.mult, ALU.add)
            return fn
        rrow_a = sch.add("vector", "vf", mkrr_a(),
                         waits=[mxld] + ([prev_rbce] if prev_rbce else []))
        rrow = sch.add("vector", "vf",
                       (lambda hw=hw: (lambda eng: eng.tensor_copy(
                           E[0:1, RRB0:RRB0 + hw],
                           SROW[0:1, R_ROW0:R_ROW0 + hw])))(),
                       waits=[rrow_a])
        prev_mxuse = rrow

        def mkrbc(hw=hw):
            def fn(eng):
                return eng.matmul(PS3[:, 0:hw], cstb[0:1, ONB:ONB + 128],
                                  E[0:1, RRB0:RRB0 + hw], start=True, stop=True)
            return fn
        rbc = sch.add("tensor", "tf", mkrbc(),
                      waits=[rrow] + ([prev_s2e] if prev_s2e else []))

        def mkrbce(hw=hw):
            def fn(eng):
                return eng.tensor_copy(Ef[:, RBCf0:RBCf0 + hw], PS3[:, 0:hw])
            return fn
        rbce = sch.add("vector", "vf", mkrbce(), waits=[rbc])
        prev_rbce = rbce

        h1evs, conv_steps, ct0s = [], [], []
        pend_cv = None
        for h in range(32):
            wmm = list(padsteps) if h == 0 else []
            if ci == 0 and h == 0:
                wmm += p2_done
            if h == 0 and prev_w2ev is not None:
                wmm.append(prev_w2ev)
            if h >= 2:
                wmm.append(h1evs[h - 2])

            def mkw1(h=h, hw=hw):
                def fn(eng):
                    ps = (PS0 if h % 2 == 0 else PS1)[:, 0:hw]
                    last = None
                    for dc in range(8):
                        src = A if dc < 4 else Bb
                        last = eng.matmul(
                            ps, src[:, (dc % 4) * 4096 + h * 128:(dc % 4) * 4096 + (h + 1) * 128],
                            E[:, G1SL0 + dc * hw:G1SL0 + (dc + 1) * hw],
                            start=(dc == 0), stop=(dc == 7))
                    return last
                return fn
            mm = sch.add("tensor", "tf", mkw1(), waits=wmm)

            hb = H1S0 if h % 2 == 0 else H1SB0

            def mkh1(h=h, hw=hw, hb=hb):
                def fn(eng):
                    return eng.tensor_mul(E[:, hb:hb + hw],
                                          (PS0 if h % 2 == 0 else PS1)[:, 0:hw],
                                          Ef[:, RBCf0:RBCf0 + hw])
                return fn
            wv = [mm, rbce] + ([conv_steps[h - 2]] if h >= 2 else [])
            h1ev = sch.add("vector", "vf", mkh1(), waits=wv)
            h1evs.append(h1ev)

            tb = T0B0 + (h % 2) * 258

            def mkct0(h=h, tw=tw, hb=hb, tb=tb):
                def fn(eng):
                    return eng.activation(Ef[:, tb:tb + tw], E[:, hb:hb + tw],
                                          ACT.Copy, bias=0.0,
                                          scale=cst[:, CW0 + h:CW0 + h + 1])
                return fn
            ct0 = sch.add("scalar", "sf", mkct0(), waits=[h1ev, st_cw])
            ct0s.append(ct0)

            if pend_cv is not None:
                conv_steps.append(sch.add("vector", "vf", pend_cv[0],
                                          waits=pend_cv[1]))

            def mkcv(h=h, tw=tw, hb=hb, tb=tb):
                def fn(eng):
                    o = h * 256
                    eng.scalar_tensor_tensor(Ef[:, YW3f0:YW3f0 + tw],
                                             E[:, hb + 1:hb + 1 + tw],
                                             cst[:, CW1 + h:CW1 + h + 1],
                                             Ef[:, tb:tb + tw],
                                             ALU.mult, ALU.add)
                    return eng.scalar_tensor_tensor(CONVO[:, o:o + tw],
                                                    E[:, hb + 2:hb + 2 + tw],
                                                    cst[:, CW2 + h:CW2 + h + 1],
                                                    Ef[:, YW3f0:YW3f0 + tw],
                                                    ALU.mult, ALU.add)
                return fn
            pend_cv = (mkcv(), [ct0])
        conv_steps.append(sch.add("vector", "vf", pend_cv[0], waits=pend_cv[1]))
        prev_h1last = h1evs[-1]

        def mksilu(tw=tw):
            def fn(eng):
                last = None
                for h in range(32):
                    o = h * 256
                    last = eng.activation(CONVO[:, o:o + tw], CONVO[:, o:o + tw],
                                          ACT.Silu,
                                          bias=cst[:, CB0 + h:CB0 + h + 1],
                                          scale=scal[:, C_ONE:C_ONE + 1])
                return last
            return fn
        silu = sch.add("scalar", "sf", mksilu(), waits=[conv_steps[-1]])
        if "a1" in taps and ci == 0:
            sch.add("sync", "do", dma([(taps["a1"], CONVO[:, 0:8192])]), waits=[silu])

        absteps = []
        mxacc = None
        for h in range(32):
            def mkabs(h=h, tw=tw):
                def fn(eng):
                    return eng.activation(Ef[:, ABSB0 + (h % 2) * 256:
                                             ABSB0 + (h % 2) * 256 + tw],
                                          CONVO[:, h * 256:h * 256 + tw], ACT.Abs)
                return fn
            wab = [silu] if h < 2 else [silu, absteps[h - 2][1]]
            ab = sch.add("scalar", "sf", mkabs(), waits=wab)

            def mkmx(h=h, tw=tw):
                def fn(eng):
                    src = Ef[:, ABSB0 + (h % 2) * 256:ABSB0 + (h % 2) * 256 + tw]
                    if h == 0:
                        return eng.tensor_copy(Ef[:, MMXf0:MMXf0 + tw], src)
                    return eng.tensor_max(Ef[:, MMXf0:MMXf0 + tw],
                                          Ef[:, MMXf0:MMXf0 + tw], src)
                return fn
            mx_ = sch.add("vector", "vf", mkmx(), waits=[ab])
            absteps.append((ab, mx_))
            mxacc = mx_

        def mkmxt(tw=tw):
            def fn(eng):
                last = None
                for j in range(tw // 128):
                    last = eng.matmul(PS3[:, j * 128:(j + 1) * 128],
                                      Ef[:, MMXf0 + j * 128:MMXf0 + (j + 1) * 128],
                                      cst[:, IDF:IDF + 128], start=True, stop=True)
                return last
            return fn
        mxt = sch.add("tensor", "tf", mkmxt(), waits=[mxacc])

        def mkmxred_b(tw=tw):
            def fn(eng):
                last = None
                for j in range(tw // 128):
                    last = eng.reduce_max(STAT[:, 30 + j:31 + j],
                                          PS3[:, j * 128:(j + 1) * 128], AX)
                return last
            return fn
        mxred_b = sch.add("vector", "vf", mkmxred_b(), waits=[mxt])

        def mkmxred_c(tw=tw):
            def fn(eng):
                last = None
                for j in range(tw // 128):
                    last = eng.tensor_single_scalar(STAT[:, 28 + j:29 + j],
                                                    STAT[:, 30 + j:31 + j],
                                                    1e-5, ALU.max)
                return last
            return fn
        mxred = sch.add("vector", "vf", mkmxred_c(), waits=[mxred_b])

        def mkr2(tw=tw):
            def fn(eng):
                last = None
                for j in range(tw // 128):
                    last = eng.tensor_scalar(STAT[:, 18 + j:19 + j],
                                             STAT[:, 28 + j:29 + j],
                                             scal[:, C_M2R:C_M2R + 1], 0.0,
                                             ALU.mult, ALU.add)
                return last
            return fn
        r2c = sch.add("vector", "vf", mkr2(), waits=[mxred])

        def mks2(tw=tw):
            def fn(eng):
                last = None
                for j in range(tw // 128):
                    eng.activation(STAT[:, 28 + j:29 + j], STAT[:, 28 + j:29 + j],
                                   ACT.Ln)
                    last = eng.activation(STAT[:, 28 + j:29 + j],
                                          STAT[:, 28 + j:29 + j], ACT.Exp,
                                          bias=scal[:, C_LN127:C_LN127 + 1],
                                          scale=scal[:, C_NEG1:C_NEG1 + 1])
                return last
            return fn
        s2c = sch.add("scalar", "sf", mks2(), waits=[r2c])

        def mks2t(tw=tw):
            def fn(eng):
                last = None
                for j in range(tw // 128):
                    last = eng.matmul(PS3[0:1, j * 128:(j + 1) * 128],
                                      STAT[:, 28 + j:29 + j],
                                      cst[:, IDF:IDF + 128], start=True, stop=True)
                return last
            return fn
        s2t = sch.add("tensor", "tf", mks2t(), waits=[s2c])

        def mks2hl_a(tw=tw):
            def fn(eng):
                eng.tensor_copy(SROW[0:1, 0:tw], PS3[0:1, 0:tw])
                return eng.tensor_copy(E[0:1, S2HI0:S2HI0 + tw], PS3[0:1, 0:tw])
            return fn
        s2hl_a = sch.add("vector", "vf", mks2hl_a(), waits=[s2t])

        def mks2hl_b(tw=tw):
            def fn(eng):
                return eng.tensor_sub(SROW[32:33, 0:tw], SROW[0:1, 0:tw],
                                      E[0:1, S2HI0:S2HI0 + tw])
            return fn
        s2hl_b = sch.add("vector", "vf", mks2hl_b(), waits=[s2hl_a])
        s2hl = sch.add("vector", "vf",
                       (lambda tw=tw: (lambda eng: eng.tensor_copy(
                           E[0:1, S2LO0:S2LO0 + tw], SROW[32:33, 0:tw])))(),
                       waits=[s2hl_b])

        def mks2bc(tw=tw):
            def fn(eng):
                eng.matmul(PS3[:, 0:tw], cstb[0:1, ONB:ONB + 128],
                           E[0:1, S2HI0:S2HI0 + tw], start=True, stop=False)
                return eng.matmul(PS3[:, 0:tw], cstb[0:1, ONB:ONB + 128],
                                  E[0:1, S2LO0:S2LO0 + tw], start=False, stop=True)
            return fn
        s2bc = sch.add("tensor", "tf", mks2bc(), waits=[s2hl])

        def mks2e(tw=tw):
            def fn(eng):
                return eng.tensor_copy(Ef[:, S2Ef0:S2Ef0 + tw], PS3[:, 0:tw])
            return fn
        s2e = sch.add("vector", "vf", mks2e(), waits=[s2bc])
        prev_s2e = s2e
        if "s2" in taps and ci == 0:
            sch.add("sync", "do", dma([(taps["s2"], Ef[:, S2Ef0:S2Ef0 + 256])]),
                    waits=[s2e])

        def mkg2(tw=tw):
            def fn(eng):
                last = None
                for h in range(32):
                    o = h * 256
                    eng.tensor_mul(Ef[:, YW3f0:YW3f0 + tw], CONVO[:, o:o + tw],
                                   Ef[:, S2Ef0:S2Ef0 + tw])
                    last = eng.tensor_scalar(Dd[:, h * 512:h * 512 + tw],
                                             Ef[:, YW3f0:YW3f0 + tw],
                                             MAGIC, -MAGIC, ALU.add, ALU.add)
                return last
            return fn
        g2 = sch.add("vector", "vf", mkg2(), waits=[s2e])
        if "g2" in taps and ci == 0:
            sch.add("gpsimd", "do", dma(
                [(taps["g2"][:, h * 256:h * 256 + 256], Dd[:, h * 512:h * 512 + 256])
                 for h in range(32)]), waits=[g2])

        for sub in range(tw // 128):
            gi = len(w2evs)
            co = (gi % 2) * 512

            def mkw2(sub=sub, co=co):
                def fn(eng):
                    last = None
                    for half in range(2):
                        ps = (PS0 if half == 0 else PS1)[:, co:co + 512]
                        for h in range(32):
                            last = eng.matmul(
                                ps, Dd[:, h * 512 + sub * 128:h * 512 + (sub + 1) * 128],
                                C[:, h * 1024 + half * 512:h * 1024 + (half + 1) * 512],
                                start=(h == 0), stop=(h == 31))
                    return last
                return fn
            w2w = [g2] + ([w2evs[gi - 2]] if gi >= 2 else [])
            w2mm = sch.add("tensor", "tf", mkw2(), waits=w2w)

            def mky2(sub=sub, co=co):
                def fn(eng):
                    eng.scalar_tensor_tensor(
                        WOTf[:, Y2W0:Y2W0 + 512], PS0[:, co:co + 512],
                        STAT[:, 18 + sub:19 + sub],
                        WOTf[:, X2SL0 + sub * 1024:X2SL0 + sub * 1024 + 512],
                        ALU.mult, ALU.add)
                    return eng.scalar_tensor_tensor(
                        WOTf[:, Y2W0 + 512:Y2W0 + 1024], PS1[:, co:co + 512],
                        STAT[:, 18 + sub:19 + sub],
                        WOTf[:, X2SL0 + sub * 1024 + 512:X2SL0 + (sub + 1) * 1024],
                        ALU.mult, ALU.add)
                return fn
            y2 = sch.add("vector", "vf", mky2(), waits=[w2mm, x2lds[sub]])
            w2evs.append(y2)
            prev_w2ev = y2

            bn2a = sch.add("vector", "vf",
                           lambda eng: eng.reduce_sum(STAT[:, 20:21],
                                                      WOTf[:, Y2W0:Y2W0 + 1024], AX),
                           waits=[y2])
            bn2b = sch.add("vector", "vf",
                           lambda eng: eng.tensor_single_scalar(
                               STAT[:, ST_MEAN:ST_MEAN + 1], STAT[:, 20:21],
                               1.0 / D, ALU.mult), waits=[bn2a])

            def mkbn2_c(eng):
                eng.tensor_scalar(WOTf[:, OUTW0:OUTW0 + 1024],
                                  WOTf[:, Y2W0:Y2W0 + 1024],
                                  STAT[:, ST_MEAN:ST_MEAN + 1], 0.0,
                                  ALU.subtract, ALU.add)
                eng.tensor_mul(WOTf[:, Y2W0:Y2W0 + 1024],
                               WOTf[:, OUTW0:OUTW0 + 1024],
                               WOTf[:, OUTW0:OUTW0 + 1024])
                return eng.reduce_sum(STAT[:, 21:22], WOTf[:, Y2W0:Y2W0 + 1024], AX)
            bn2c = sch.add("vector", "vf", mkbn2_c, waits=[bn2b])
            bn2 = sch.add("vector", "vf",
                          lambda eng: eng.tensor_single_scalar(
                              STAT[:, ST_VAR:ST_VAR + 1], STAT[:, 21:22],
                              1.0 / D, ALU.mult), waits=[bn2c])

            def mkrstd2(eng):
                eng.activation(STAT[:, ST_RSTD:ST_RSTD + 1],
                               STAT[:, ST_VAR:ST_VAR + 1], ACT.Ln,
                               bias=scal[:, C_TINY:C_TINY + 1],
                               scale=scal[:, C_ONE:C_ONE + 1])
                return eng.activation(STAT[:, ST_RSTD:ST_RSTD + 1],
                                      STAT[:, ST_RSTD:ST_RSTD + 1], ACT.Exp,
                                      bias=scal[:, C_ZERO:C_ZERO + 1],
                                      scale=scal[:, C_NHALF:C_NHALF + 1])
            rstd2 = sch.add("scalar", "sf", mkrstd2, waits=[bn2])

            def mkoutn(eng):
                return eng.activation(WOTf[:, OUTW0:OUTW0 + 1024],
                                      WOTf[:, OUTW0:OUTW0 + 1024],
                                      ACT.Copy, bias=0.0,
                                      scale=STAT[:, ST_RSTD:ST_RSTD + 1])
            outn = sch.add("scalar", "sf", mkoutn,
                           waits=[rstd2] + ([out_steps[-1]] if out_steps else []))
            prev_out = outn

            out_steps.append(sch.add("sync", "do", dma(
                [(d_y[t0 + sub * 128:t0 + (sub + 1) * 128, :],
                  WOTf[:, OUTW0:OUTW0 + 1024])]), waits=[outn]))

    if _PREFIX is not None:
        sch.steps = sch.steps[:_PREFIX]
    do_total = sum(s.n_inc for s in sch.steps if s.sem == "do")

    def fin(eng):
        return eng.nop()
    last = sch.add("sync", "dz", fin, waits=[_FW("do", do_total)])

    sch.plan()
    with nc.Block() as blk:
        sch.emit(pr, blk)
    return pr, sch, taps


class _FW:
    def __init__(self, sem, count):
        self.sem = sem
        self.count = count


_BUILT = None


def _get_built():
    global _BUILT
    if _BUILT is None:
        _BUILT = build()
    return _BUILT


def kernel(L0_attn_x, L0_attn_wq, L0_attn_wk_a, L0_attn_wk_b, L0_attn_wv_a,
           L0_attn_wv_b, L0_attn_wo, L0_ffn_w1, L0_ffn_conv_w, L0_ffn_conv_b,
           L0_ffn_w2):
    pr, sch, taps = _get_built()
    consts = _host_consts()
    x = np.ascontiguousarray(np.asarray(L0_attn_x, dtype=np.float32))
    w = {
        "wq": np.ascontiguousarray(L0_attn_wq, dtype=np.float32),
        "wka": np.ascontiguousarray(L0_attn_wk_a, dtype=np.float32),
        "wkb": np.ascontiguousarray(L0_attn_wk_b, dtype=np.float32),
        "wva": np.ascontiguousarray(L0_attn_wv_a, dtype=np.float32),
        "wvb": np.ascontiguousarray(L0_attn_wv_b, dtype=np.float32),
        "wo": np.ascontiguousarray(L0_attn_wo, dtype=np.float32),
        "w1": np.ascontiguousarray(L0_ffn_w1, dtype=np.float32),
        "cw": np.ascontiguousarray(
            np.asarray(L0_ffn_conv_w, dtype=np.float32).reshape(DH, 3)),
        "cb": np.ascontiguousarray(L0_ffn_conv_b, dtype=np.float32),
        "w2": np.ascontiguousarray(L0_ffn_w2, dtype=np.float32),
    }
    in_maps = []
    for c in range(8):
        b, half = divmod(c, 2)
        qlo = 0 if half == 0 else S - LQ
        m = dict(w)
        m["xq"] = np.ascontiguousarray(x[b, qlo:qlo + LQ, :])
        m["xf"] = np.ascontiguousarray(x[b])
        m.update(consts)
        in_maps.append(m)
    res = run_bass_kernel_spmd(pr.nc, in_maps, core_ids=list(range(8)))
    out = np.empty((B, S, D), dtype=np.float32)
    for c in range(8):
        b, half = divmod(c, 2)
        ownstart = 0 if half == 0 else 128
        y = np.asarray(res.results[c]["y"])
        out[b, half * 2048:(half + 1) * 2048, :] = y[ownstart:ownstart + 2048, :]
    if _TAPS:
        kernel.taps = [{k: np.asarray(res.results[c]["tap_" + k]) for k in _TAPS}
                       for c in range(8)]
    return out


# version 14
# speedup vs baseline: 1.5398x; 1.0014x over previous
import numpy as np
import ml_dtypes

import concourse.bass as bass
import concourse.mybir as mybir
from concourse.bass_utils import run_bass_kernel_spmd

F32 = mybir.dt.float32
BF16 = mybir.dt.bfloat16
AX = mybir.AxisListType.X
ALU = mybir.AluOpType
ACT = mybir.ActivationFunctionType

B, S, D = 4, 4096, 1024
NH, HD, DROPE, LAT, DH = 16, 64, 16, 256, 4096
NPAIR = NH // 2
QTN = 17
LQ = QTN * 128
SKT = S // 128
MAGIC = float(np.float32(1.5 * 2.0 ** 23))
EXP_SHIFT = 25.0
LN127 = float(np.log(127.0))
QBLOCKS = [(0, 512), (512, 512), (1024, 512), (1536, 512), (2048, 128)]
FFN_CHUNKS = [(k * 256, 256) for k in range(8)] + [(2048, 128)]

_TAPS = []
_PREFIX = None

ENGS = ("tensor", "scalar", "vector", "gpsimd", "sync")


class Step:
    __slots__ = ("eng", "sem", "fn", "waits", "n_inc", "self_inc", "count",
                 "pairs", "slow")

    def __init__(self, eng, sem, fn, waits, n_inc, self_inc):
        self.eng, self.sem, self.fn = eng, sem, fn
        self.waits, self.n_inc, self.self_inc = list(waits), n_inc, self_inc
        self.count = None
        self.pairs = None
        self.slow = False


class Sched:
    def __init__(self):
        self.steps = []

    def add(self, eng, sem, fn, waits=(), n_inc=1, self_inc=False):
        if isinstance(fn, tuple) and fn[0] == "dma":
            st = Step(eng, sem, None, [w for w in waits if w is not None],
                      len(fn[1]), self_inc)
            st.pairs = fn[1]
            st.slow = (len(fn) > 2 and fn[2])
        else:
            st = Step(eng, sem, fn, [w for w in waits if w is not None],
                      n_inc, self_inc)
        self.steps.append(st)
        return st

    def plan(self):
        counts = {}
        for st in self.steps:
            counts[st.sem] = counts.get(st.sem, 0) + st.n_inc
            st.count = counts[st.sem]
        self.sem_names = sorted(counts)

    def emit(self, pr, blk):
        sems = {}
        for name in self.sem_names:
            g = pr.nc.semaphore("sem_" + name)
            sems[name] = g.__enter__()
            pr.ctx.append(g)
        by_eng = {e: [st for st in self.steps if st.eng == e] for e in ENGS}

        def mk(engname):
            def body(eng):
                for st in by_eng[engname]:
                    seen = {}
                    for w in st.waits:
                        ws, wc = w.sem, w.count
                        if seen.get(ws, -1) >= wc:
                            continue
                        seen[ws] = wc
                        v = wc * (16 if ws.startswith("d") else 1)
                        if v > 0:
                            eng.wait_ge(sems[ws], v)
                    if st.pairs is not None:
                        for dst, srcp in st.pairs:
                            i = (eng.dma_start(dst, srcp,
                                               allow_slow_non_contiguous=True)
                                 if st.slow else eng.dma_start(dst, srcp))
                            i.then_inc(sems[st.sem], 16)
                    else:
                        last = st.fn(eng)
                        if not st.self_inc:
                            assert last is not None, (st.sem, st.eng)
                            last.then_inc(sems[st.sem],
                                          16 * st.n_inc if st.sem.startswith("d")
                                          else st.n_inc)
            return body

        blk.tensor(mk("tensor"))
        blk.scalar(mk("scalar"))
        blk.vector(mk("vector"))
        blk.gpsimd(mk("gpsimd"))
        blk.sync(mk("sync"))


class Prog:
    def __init__(self):
        self.nc = bass.Bass("TRN2", target_bir_lowering=False)
        self.ctx = []

    def S(self, name, shape, dt=F32):
        g = self.nc.sbuf_tensor(name, shape, dt)
        t = g.__enter__()
        self.ctx.append(g)
        return t

    def P(self, name, shape, dt=F32):
        g = self.nc.psum_tensor(name, shape, dt)
        t = g.__enter__()
        self.ctx.append(g)
        return t


def _host_consts():
    identf = np.eye(128, dtype=np.float32)
    identb = np.eye(128, dtype=ml_dtypes.bfloat16)
    onesb = np.ones((128, 128), dtype=ml_dtypes.bfloat16)
    inv = 10000.0 ** (-(np.arange(0, HD, 2, dtype=np.float64)[: DROPE // 2]) / HD)
    rt = np.zeros((NPAIR, 128, 128), dtype=np.float64)
    for p in range(NPAIR):
        for sub, h in enumerate((2 * p, 2 * p + 1)):
            base = sub * 64
            blk = np.eye(64)
            for j in range(DROPE // 2):
                c = np.cos(h * inv[j]); s = np.sin(h * inv[j])
                blk[2 * j, 2 * j] = c;     blk[2 * j, 2 * j + 1] = -s
                blk[2 * j + 1, 2 * j] = s; blk[2 * j + 1, 2 * j + 1] = c
            rt[p, base:base + 64, base:base + 64] = blk.T
    rtf = np.ascontiguousarray(
        rt.astype(np.float32).transpose(1, 0, 2).reshape(128, NPAIR * 128))
    return {"identf": identf, "identb": identb, "onesb": onesb, "rt": rtf}


def build():
    pr = Prog()
    nc = pr.nc
    sch = Sched()

    DI = lambda n, s, dt=F32: nc.dram_tensor(n, s, dt, kind="ExternalInput").ap()
    d_xq = DI("xq", [LQ, D]); d_xf = DI("xf", [S, D])
    d_wq = DI("wq", [D, D]); d_wka = DI("wka", [LAT, D]); d_wkb = DI("wkb", [D, LAT])
    d_wva = DI("wva", [LAT, D]); d_wvb = DI("wvb", [D, LAT]); d_wo = DI("wo", [D, D])
    d_w1 = DI("w1", [DH, D]); d_cw = DI("cw", [DH, 3]); d_cb = DI("cb", [DH])
    d_w2 = DI("w2", [D, DH])
    d_identf = DI("identf", [128, 128]); d_identb = DI("identb", [128, 128], BF16)
    d_onesb = DI("onesb", [128, 128], BF16); d_rt = DI("rt", [128, NPAIR * 128])
    d_y = nc.dram_tensor("y", [LQ, D], F32, kind="ExternalOutput").ap()
    d_x2sp = nc.dram_tensor("x2sp", [LQ, D], F32).ap()
    d_g1sp = nc.dram_tensor("g1sp", [8, 128, LQ], BF16).ap()
    d_qsp = nc.dram_tensor("qsp", [NPAIR, 128, LQ], BF16).ap()
    d_mxsp = nc.dram_tensor("mxsp", [1, LQ], F32).ap()

    taps = {}
    for t in _TAPS:
        shp, dt = {
            "attn": ([LQ, D], F32), "x2": ([LQ, D], F32),
            "kchk": ([128, S], F32), "qchk": ([128, LQ], F32),
            "vchk": ([128, 1024], F32), "den": ([128, 512], F32),
            "g1": ([8, 128, LQ], F32), "a1": ([128, 8192], F32),
            "g2": ([128, 8192], F32), "t1": ([128, 4096], F32),
            "t2": ([128, 1024], F32), "s2": ([128, 256], F32),
            "mx": ([1, LQ], F32), "usb": ([128, 512], F32),
            "yw": ([128, 1024], F32), "stt": ([128, 72], F32),
            "x2w": ([128, 1024], F32),
        }[t]
        taps[t] = nc.dram_tensor("tap_" + t, shp, dt, kind="ExternalOutput").ap()

    A = pr.S("A", [128, 16384], BF16)
    Bb = pr.S("Bb", [128, 16384], BF16)
    C = pr.S("C", [128, 32768], BF16)
    Dd = pr.S("Dd", [128, 16384], BF16)
    E = pr.S("E", [128, 10240], BF16)
    WOT = pr.S("WOT", [128, 8192], BF16)
    SROW = pr.S("SROW", [128, 1024])
    cst = pr.S("cst", [128, 256 + NPAIR * 128 + 128])
    IDF, RT0 = 0, 256
    cstb = pr.S("cstb", [128, 256], BF16)
    IDB, ONB = 0, 128
    scal = pr.S("scal", [128, 16])
    (C_EXPB, C_EXPS, C_NEG1, C_LN127, C_MAGIC, C_NMAGIC, C_TINY, C_M1R,
     C_SW1, C_M2R, C_SW2, C_NHALF, C_ZERO, C_ONE, C_X1, C_X2) = range(16)
    STAT = pr.S("STAT", [128, 72])
    ST_MEAN, ST_VAR, ST_RSTD, ST_MX, ST_S1 = 12, 13, 14, 16, 17
    ST_W = 32

    Af = A.bitcast(F32)
    Bf = Bb.bitcast(F32)
    Df = Dd.bitcast(F32)
    Ef = E.bitcast(F32)
    WOTf = WOT.bitcast(F32)

    PS0 = pr.P("PS0", [128, 1024])
    PS1 = pr.P("PS1", [128, 1024])
    PS2 = pr.P("PS2", [128, 1024])
    PS3 = pr.P("PS3", [128, 512])
    PSB = pr.P("PSB", [128, 1024], BF16)

    WSTG = Bf
    WQT = Dd
    XTOKB = Dd
    XTB0 = 8192
    WVAT0, WVBT0 = 10240, 12288
    SM_KA, SM_KB = 0, 2048
    KLB0 = 4096
    VLT0 = 0
    XSTG = Af
    XQT0 = 4096
    QSTG0 = 8192 + 1024

    dma = lambda pairs: ("dma", pairs)
    dma_slow = lambda pairs: ("dma", pairs, True)

    st_cst = sch.add("sync", "dw", dma([
        (cst[:, IDF:IDF + 128], d_identf),
        (cst[:, RT0:RT0 + NPAIR * 128], d_rt),
        (cstb[:, IDB:IDB + 128], d_identb),
        (cstb[:, ONB:ONB + 128], d_onesb)]))

    def memset_consts(eng):
        last = None
        for idx, val in [(C_EXPB, -EXP_SHIFT), (C_EXPS, 0.125), (C_NEG1, -1.0),
                         (C_LN127, LN127), (C_MAGIC, MAGIC), (C_NMAGIC, -MAGIC),
                         (C_TINY, 1e-5), (C_NHALF, -0.5), (C_ZERO, 0.0),
                         (C_ONE, 1.0), (C_X1, float(np.log(DH * D)))]:
            last = eng.memset(scal[:, idx:idx + 1], val)
        return last
    st_scal = sch.add("vector", "vv", memset_consts)

    def fold(dram, ntiles, row_free, rhs_sel, out_sel, waits):
        stg = sch.add("sync", "dw", dma([
            (WSTG[:, 0:ntiles * row_free],
             bass.AP(dram.tensor, 0,
                     [[row_free, 128], [128 * row_free, ntiles], [1, row_free]]))]),
            waits=waits)
        evs, mms = [], []
        ncc = row_free // 128

        def fold_bank(k):
            t_, o_ = (PS0, 0) if k % 4 == 0 else (PS0, 512) if k % 4 == 1 else                      (PS1, 0) if k % 4 == 2 else (PS1, 512)
            return t_[:, o_:o_ + 128]
        k = 0
        for ti in range(ntiles):
            for cc in range(ncc):
                w = [stg, st_cst] + ([evs[k - 4]] if k >= 4 else [])

                def mkm(ti=ti, cc=cc, k=k):
                    def fn(eng):
                        return eng.matmul(
                            fold_bank(k),
                            WSTG[:, ti * row_free + cc * 128:
                                 ti * row_free + (cc + 1) * 128],
                            rhs_sel(ti, cc), start=True, stop=True)
                    return fn
                mm = sch.add("tensor", "tt", mkm(), waits=w)
                mms.append(mm)

                def mke(ti=ti, cc=cc, k=k):
                    def fn(eng):
                        dst, off = out_sel(ti, cc)
                        return eng.tensor_copy(dst[:, off:off + 128], fold_bank(k))
                    return fn
                evs.append(sch.add("vector", "vv", mke(), waits=[mm]))
                k += 1
        return stg, mms, evs

    ident_rhs = lambda ti, cc: cst[:, IDF:IDF + 128]
    rope_rhs = lambda ti, cc: cst[:, RT0 + ti * 128:RT0 + (ti + 1) * 128]

    stg_q, mm_q, ev_q = fold(d_wq, 8, 1024, rope_rhs,
                             lambda ti, cc: (WQT, cc * 1024 + ti * 128),
                             waits=[st_cst])

    xload, xcast, xtr, xev = [], [], [], []
    for t in range(SKT):
        buf = t % 2
        w = ([xcast[t - 2]] if t >= 2 else [ev_q[-1]])
        stl = sch.add("sync", "dx",
                      dma([(XSTG[:, buf * 1024:(buf + 1) * 1024],
                            d_xf[t * 128:(t + 1) * 128, :])]), waits=w)
        xload.append(stl)

        def mkc(buf=buf):
            def fn(eng):
                return eng.tensor_copy(Dd[:, XTB0 + buf * 1024:XTB0 + (buf + 1) * 1024],
                                       XSTG[:, buf * 1024:(buf + 1) * 1024])
            return fn
        xcast.append(sch.add("vector", "vc", mkc(), waits=[stl]))
        w2 = [xcast[t]] + ([xev[t - 1]] if t >= 1 else [])

        def mkt(buf=buf):
            def fn(eng):
                last = None
                for dc in range(8):
                    last = eng.transpose(
                        PSB[:, dc * 128:(dc + 1) * 128],
                        Dd[:, XTB0 + buf * 1024 + dc * 128:XTB0 + buf * 1024 + (dc + 1) * 128],
                        cstb[:, IDB:IDB + 128])
                return last
            return fn
        xtr.append(sch.add("tensor", "tt", mkt(), waits=w2))

        def mke(t=t):
            def fn(eng):
                last = None
                for dc in range(8):
                    last = eng.tensor_copy(
                        C[:, dc * S + t * 128:dc * S + (t + 1) * 128],
                        PSB[:, dc * 128:(dc + 1) * 128])
                return last
            return fn
        xev.append(sch.add("vector", "vv", mke(), waits=[xtr[t]]))

    halves = [(0, 9), (9, 8)]
    qspills = []
    prev_half_done = None
    for hi, (t0, nt) in enumerate(halves):
        hw = nt * 128
        hx_tr, hx_ev = [], []
        for j in range(nt):
            ti = t0 + j
            buf = j % 2
            wl = [hx_ev[j - 2]] if j >= 2 else \
                ([prev_half_done] if prev_half_done else [xcast[SKT - 1], xcast[SKT - 2]])
            stl = sch.add("sync", "dx",
                          dma([(XSTG[:, buf * 1024:(buf + 1) * 1024],
                                d_xq[ti * 128:(ti + 1) * 128, :])]), waits=wl)

            def mkc(buf=buf):
                def fn(eng):
                    return eng.tensor_copy(
                        Dd[:, XTB0 + buf * 1024:XTB0 + (buf + 1) * 1024],
                        XSTG[:, buf * 1024:(buf + 1) * 1024])
                return fn
            cp = sch.add("vector", "vc", mkc(),
                         waits=[stl] + ([hx_tr[j - 1]] if j >= 1 else
                                        ([xtr[SKT - 1]] if hi == 0 else [])))

            def mkt(buf=buf):
                def fn(eng):
                    last = None
                    for dc in range(8):
                        last = eng.transpose(
                            PSB[:, dc * 128:(dc + 1) * 128],
                            Dd[:, XTB0 + buf * 1024 + dc * 128:
                               XTB0 + buf * 1024 + (dc + 1) * 128],
                            cstb[:, IDB:IDB + 128])
                    return last
                return fn
            wtr = [cp] + ([hx_ev[j - 1]] if j >= 1 else
                          ([xev[SKT - 1]] if hi == 0 else [prev_half_done]))
            tr = sch.add("tensor", "tt", mkt(), waits=wtr)
            hx_tr.append(tr)

            def mke(j=j, hw=hw):
                def fn(eng):
                    last = None
                    for dc in range(8):
                        last = eng.tensor_copy(
                            A[:, XQT0 + dc * hw + j * 128:XQT0 + dc * hw + (j + 1) * 128],
                            PSB[:, dc * 128:(dc + 1) * 128])
                    return last
                return fn
            hx_ev.append(sch.add("vector", "vv", mke(), waits=[tr]))

        blocks = []
        q0 = 0
        while q0 < hw:
            blocks.append((q0, min(512, hw - q0)))
            q0 += 512
        hq_ev = []
        for p in range(NPAIR):
            for bi, (q0, qw) in enumerate(blocks):
                k = p * len(blocks) + bi
                w = [hx_ev[-1], ev_q[-1]] + ([hq_ev[k - 2]] if k >= 2 else [])

                def mkm(p=p, q0=q0, qw=qw, k=k, hw=hw):
                    def fn(eng):
                        ps = (PS0 if k % 2 == 0 else PS1)[:, 0:qw]
                        last = None
                        for dc in range(8):
                            last = eng.matmul(
                                ps, WQT[:, dc * 1024 + p * 128:dc * 1024 + (p + 1) * 128],
                                A[:, XQT0 + dc * hw + q0:XQT0 + dc * hw + q0 + qw],
                                start=(dc == 0), stop=(dc == 7))
                        return last
                    return fn
                mm = sch.add("tensor", "tt", mkm(), waits=w)

                def mke(k=k, qw=qw):
                    def fn(eng):
                        return eng.tensor_copy(E[:, QSTG0 + (k % 2) * 512:
                                                 QSTG0 + (k % 2) * 512 + qw],
                                               (PS0 if k % 2 == 0 else PS1)[:, 0:qw])
                    return fn
                ev = sch.add("vector", "vv", mke(), waits=[mm] +
                             ([qspills[-2]] if len(qspills) >= 2 else []))
                hq_ev.append(ev)

                qspills.append(sch.add("sync", "do", dma(
                    [(d_qsp[p, :, t0 * 128 + q0:t0 * 128 + q0 + qw],
                      E[:, QSTG0 + (k % 2) * 512:QSTG0 + (k % 2) * 512 + qw])]),
                    waits=[ev]))
        prev_half_done = hq_ev[-1]
    q_done = [qspills[-1], qspills[-2]]

    stg_ka, mm_ka, ev_ka = fold(d_wka, 2, 1024, ident_rhs,
                                lambda ti, cc: (E, SM_KA + cc * 256 + ti * 128),
                                waits=[prev_half_done])
    stg_kb, mm_kb, ev_kb = fold(d_wkb, 8, 256, rope_rhs,
                                lambda ti, cc: (E, SM_KB + cc * 1024 + ti * 128),
                                waits=[mm_ka[-1]])
    stg_va, mm_va, ev_va = fold(d_wva, 2, 1024, ident_rhs,
                                lambda ti, cc: (Dd, WVAT0 + cc * 256 + ti * 128),
                                waits=[mm_kb[-1], prev_half_done])
    stg_vb, mm_vb, ev_vb = fold(d_wvb, 8, 256, ident_rhs,
                                lambda ti, cc: (Dd, WVBT0 + cc * 1024 + ti * 128),
                                waits=[mm_va[-1]])
    stg_wo, mm_wo, ev_wo = fold(d_wo, 8, 1024, ident_rhs,
                                lambda ti, cc: (WOT, cc * 1024 + ti * 128),
                                waits=[mm_vb[-1]])

    def KTs(p):
        return (A, p * S) if p < 4 else (Bb, (p - 4) * S)

    kev = []
    prev_kl = None
    for sb in range(8):
        w = [ev_ka[-1], xev[-1], q_done[0], ev_wo[-1]] + ([prev_kl] if prev_kl else [])

        def mkkl(sb=sb):
            def fn(eng):
                last = None
                for lc in range(2):
                    for dc in range(8):
                        last = eng.matmul(
                            PS2[:, lc * 512:(lc + 1) * 512],
                            E[:, SM_KA + dc * 256 + lc * 128:SM_KA + dc * 256 + (lc + 1) * 128],
                            C[:, dc * S + sb * 512:dc * S + (sb + 1) * 512],
                            start=(dc == 0), stop=(dc == 7))
                return last
            return fn
        klmm = sch.add("tensor", "tt", mkkl(), waits=w)

        def mkkle(eng):
            return eng.tensor_copy(E[:, KLB0:KLB0 + 1024], PS2[:, 0:1024])
        klev = sch.add("vector", "vv", mkkle, waits=[klmm])

        kmm_l = None
        for p in range(NPAIR):
            w2 = [klev, ev_kb[-1]] + ([kev[-2]] if len(kev) >= 2 else [])

            def mkk(p=p, sb=sb):
                def fn(eng):
                    ps = (PS0 if p % 2 == 0 else PS1)[:, 0:512]
                    last = None
                    for lc in range(2):
                        last = eng.matmul(
                            ps, E[:, SM_KB + lc * 1024 + p * 128:
                                  SM_KB + lc * 1024 + (p + 1) * 128],
                            E[:, KLB0 + lc * 512:KLB0 + (lc + 1) * 512],
                            start=(lc == 0), stop=(lc == 1))
                    return last
                return fn
            kmm = sch.add("tensor", "tt", mkk(), waits=w2)
            kmm_l = kmm

            def mkke(p=p, sb=sb):
                def fn(eng):
                    dst, off = KTs(p)
                    return eng.tensor_copy(dst[:, off + sb * 512:off + (sb + 1) * 512],
                                           (PS0 if p % 2 == 0 else PS1)[:, 0:512])
                return fn
            kev.append(sch.add("vector", "vv", mkke(), waits=[kmm]))
        prev_kl = kmm_l

    vlev = []
    vlk = 0
    for lc in range(2):
        for sb in range(8):
            w = [ev_va[-1], kev[-1]] if vlk < 2 else [vlev[vlk - 2]]

            def mkvl(lc=lc, sb=sb, vlk=vlk):
                def fn(eng):
                    ps = (PS0 if vlk % 2 == 0 else PS1)[:, 0:512]
                    last = None
                    for dc in range(8):
                        last = eng.matmul(
                            ps, Dd[:, WVAT0 + dc * 256 + lc * 128:
                                   WVAT0 + dc * 256 + (lc + 1) * 128],
                            C[:, dc * S + sb * 512:dc * S + (sb + 1) * 512],
                            start=(dc == 0), stop=(dc == 7))
                    return last
                return fn
            mm = sch.add("tensor", "tt", mkvl(), waits=w)

            def mkvle(lc=lc, sb=sb, vlk=vlk):
                def fn(eng):
                    return eng.tensor_copy(
                        E[:, VLT0 + lc * S + sb * 512:VLT0 + lc * S + (sb + 1) * 512],
                        (PS0 if vlk % 2 == 0 else PS1)[:, 0:512])
                return fn
            vlev.append(sch.add("vector", "vv", mkvle(), waits=[mm]))
            vlk += 1

    vev = []
    for ti in range(SKT):
        w = [vlev[-1], ev_vb[-1]] + ([vev[ti - 2]] if ti >= 2 else [])

        def mkv(ti=ti):
            def fn(eng):
                ps = PS0 if ti % 2 == 0 else PS1
                last = None
                for half in range(2):
                    for lc in range(2):
                        last = eng.matmul(
                            ps[:, half * 512:(half + 1) * 512],
                            E[:, VLT0 + lc * S + ti * 128:VLT0 + lc * S + (ti + 1) * 128],
                            Dd[:, WVBT0 + lc * 1024 + half * 512:
                               WVBT0 + lc * 1024 + (half + 1) * 512],
                            start=(lc == 0), stop=(lc == 1))
                return last
            return fn
        mm = sch.add("tensor", "tt", mkv(), waits=w)

        def mkve(ti=ti):
            def fn(eng):
                return eng.tensor_copy(C[:, ti * 1024:(ti + 1) * 1024],
                                       (PS0 if ti % 2 == 0 else PS1)[:, :])
            return fn
        vev.append(sch.add("vector", "vv", mkve(), waits=[mm]))

    p0_done = [vev[-1], ev_wo[-1], kev[-1], st_scal]

    if "kchk" in taps:
        sch.add("gpsimd", "do", dma([(taps["kchk"], A[:, 0:S])]), waits=p0_done)
    if "qchk" in taps:
        sch.add("gpsimd", "do", dma([(taps["qchk"], d_qsp[0, :, :])]),
                waits=[qspills[-1]])
    if "vchk" in taps:
        sch.add("gpsimd", "do", dma([(taps["vchk"], C[:, 0:1024])]), waits=p0_done)

    EXPB0, USB0, QTQ0, G1T0, G1S0, MXR0 = 0, 2048, 6144, 10240, 11264, 12288
    XRS0, YW0, X2W0, RDEN0, LNB0 = 0, 1024, 2048, 3072, 3584

    resload = [None] * QTN
    lnst = {}
    uev = []
    rden_step = None
    prev_wo_evac = []
    qtq_loads = []
    last_sc_of_qb = []
    prev_exp, prev_av = None, None

    pa_items = ([("w1", t, s, t) for t in range(32) for s in range(2)] +
                [("w2", t, s, 32 + t) for t in range(8) for s in range(8)])
    pa_reds, pa_steps, pa_fin = [], [], {}

    for qbi, (q0, qw) in enumerate(QBLOCKS):
        wq_ = list(p0_done) if qbi == 0 else [last_sc_of_qb[-1]]
        qtq = sch.add("sync", "dq", dma(
            [(Dd[:, QTQ0 + p * 512:QTQ0 + p * 512 + qw], d_qsp[p, :, q0:q0 + qw])
             for p in range(NPAIR)]), waits=wq_)
        qtq_loads.append(qtq)

        pair_u = []
        for p in range(NPAIR):
            sc_steps, exp_steps, av_steps = [], [], []
            pend_av = None
            for kt in range(SKT):
                w = []
                if kt == 0:
                    w.append(qtq)
                    if prev_exp is not None:
                        w.append(prev_exp[SKT - 2])
                    else:
                        w += p0_done
                if kt == 1 and prev_exp is not None:
                    w.append(prev_exp[SKT - 1])
                if kt >= 2:
                    w.append(exp_steps[kt - 2])

                def mksc(p=p, qw=qw, kt=kt):
                    def fn(eng):
                        ps = PS0 if kt % 2 == 0 else PS1
                        ka, koff = KTs(p)
                        eng.matmul(ps[:, 0:qw],
                                   ka[0:64, koff + kt * 128:koff + (kt + 1) * 128],
                                   Dd[0:64, QTQ0 + p * 512:QTQ0 + p * 512 + qw],
                                   start=True, stop=True, tile_position=(0, 0))
                        return eng.matmul(
                            ps[:, 512:512 + qw],
                            ka[64:128, koff + kt * 128:koff + (kt + 1) * 128],
                            Dd[64:128, QTQ0 + p * 512:QTQ0 + p * 512 + qw],
                            start=True, stop=True, tile_position=(64, 0))
                    return fn
                sc = sch.add("tensor", "te", mksc(), waits=w)
                sc_steps.append(sc)

                we = [sc]
                if kt >= 2:
                    we.append(av_steps[kt - 2])
                elif prev_av is not None:
                    we.append(prev_av[SKT - 2 + kt])

                def mkexp(qw=qw, kt=kt):
                    def fn(eng):
                        ps = PS0 if kt % 2 == 0 else PS1
                        eb0 = EXPB0 + (kt % 2) * 1024
                        if qw == 512:
                            return eng.activation(
                                Dd[:, eb0:eb0 + 1024], ps[:, 0:1024], ACT.Exp,
                                bias=scal[:, C_EXPB:C_EXPB + 1],
                                scale=scal[:, C_EXPS:C_EXPS + 1])
                        eng.activation(Dd[:, eb0:eb0 + qw], ps[:, 0:qw], ACT.Exp,
                                       bias=scal[:, C_EXPB:C_EXPB + 1],
                                       scale=scal[:, C_EXPS:C_EXPS + 1])
                        return eng.activation(
                            Dd[:, eb0 + 512:eb0 + 512 + qw], ps[:, 512:512 + qw],
                            ACT.Exp, bias=scal[:, C_EXPB:C_EXPB + 1],
                            scale=scal[:, C_EXPS:C_EXPS + 1])
                    return fn
                ex = sch.add("scalar", "se", mkexp(), waits=we)
                exp_steps.append(ex)

                if pend_av is not None:
                    av_steps.append(sch.add("tensor", "te", pend_av[0],
                                            waits=pend_av[1]))

                def mkav(p=p, qw=qw, kt=kt):
                    def fn(eng):
                        eb0 = EXPB0 + (kt % 2) * 1024
                        st = (kt == 0); sp = (kt == SKT - 1)
                        eng.matmul(PS2[0:64, 0:qw],
                                   C[:, kt * 1024 + p * 128:kt * 1024 + p * 128 + 64],
                                   Dd[:, eb0:eb0 + qw], start=st, stop=sp,
                                   tile_position=(0, 0))
                        eng.matmul(PS2[64:128, 0:qw],
                                   C[:, kt * 1024 + p * 128 + 64:kt * 1024 + (p + 1) * 128],
                                   Dd[:, eb0 + 512:eb0 + 512 + qw], start=st, stop=sp,
                                   tile_position=(0, 64))
                        eng.matmul(PS2[0:64, 512:512 + qw], cstb[0:128, ONB:ONB + 64],
                                   Dd[:, eb0:eb0 + qw], start=st, stop=sp,
                                   tile_position=(0, 0))
                        return eng.matmul(PS2[64:128, 512:512 + qw],
                                          cstb[0:128, ONB:ONB + 64],
                                          Dd[:, eb0 + 512:eb0 + 512 + qw],
                                          start=st, stop=sp, tile_position=(0, 64))
                    return fn
                wa = [ex]
                if kt == 0:
                    if uev:
                        wa += [uev[-1], rden_step]
                    else:
                        wa += p0_done
                    wa += prev_wo_evac
                pend_av = (mkav(), wa)
            av_steps.append(sch.add("tensor", "te", pend_av[0],
                                    waits=pend_av[1]))
            prev_exp, prev_av = exp_steps, av_steps

            def mkrd(qw=qw):
                def fn(eng):
                    eng.activation(Ef[:, LNB0:LNB0 + qw], PS2[:, 512:512 + qw], ACT.Ln)
                    return eng.activation(Ef[:, RDEN0:RDEN0 + qw],
                                          Ef[:, LNB0:LNB0 + qw], ACT.Exp,
                                          bias=scal[:, C_ZERO:C_ZERO + 1],
                                          scale=scal[:, C_NEG1:C_NEG1 + 1])
                return fn
            rden_step = sch.add("scalar", "ss", mkrd(), waits=[av_steps[-1]])

            def mku(p=p, qw=qw):
                def fn(eng):
                    return eng.tensor_mul(Dd[:, USB0 + p * 512:USB0 + p * 512 + qw],
                                          PS2[:, 0:qw], Ef[:, RDEN0:RDEN0 + qw])
                return fn
            ue = sch.add("vector", "ve", mku(), waits=[rden_step])
            uev.append(ue)
            pair_u.append(ue)
            if "usb" in taps and qbi == 0 and p == 0:
                sch.add("gpsimd", "do",
                        dma([(taps["usb"], Dd[:, USB0:USB0 + 512])]), waits=[ue])
        last_sc_of_qb.append(sc_steps[-1])

        if "den" in taps and qbi == 0:
            sch.add("sync", "do", dma([(taps["den"], Ef[:, RDEN0:RDEN0 + 512])]),
                    waits=[pair_u[-1]])

        ntile = qw // 128
        for sub in range(ntile):
            ti = q0 // 128 + sub
            xb = XRS0 + (ti % 2) * 4096
            wr = list(p0_done) if ti < 2 else [lnst[ti - 2]["ydone"]]
            resload[ti] = sch.add("sync", "dr", dma(
                [(Ef[:, xb:xb + 1024], d_xq[ti * 128:(ti + 1) * 128, :])]),
                waits=wr)

            w = list(pair_u) if sub == 0 else [lnst[ti - 1]["ydone"]]

            def mkwo(sub=sub):
                def fn(eng):
                    last = None
                    for half in range(2):
                        for p in range(NPAIR):
                            last = eng.matmul(
                                PS2[:, half * 512:(half + 1) * 512],
                                Dd[:, USB0 + p * 512 + sub * 128:USB0 + p * 512 + (sub + 1) * 128],
                                WOT[:, p * 1024 + half * 512:p * 1024 + (half + 1) * 512],
                                start=(p == 0), stop=(p == NPAIR - 1))
                    return last
                return fn
            wo = sch.add("tensor", "tt", mkwo(), waits=w)

            def mky(xb=xb):
                def fn(eng):
                    eng.tensor_add(Ef[:, YW0:YW0 + 512], PS2[:, 0:512],
                                   Ef[:, xb:xb + 512])
                    return eng.tensor_add(Ef[:, YW0 + 512:YW0 + 1024],
                                          PS2[:, 512:1024],
                                          Ef[:, xb + 512:xb + 1024])
                return fn
            ydone = sch.add("vector", "vv", mky(), waits=[wo, resload[ti]])
            ywtap = None
            if "yw" in taps and ti == 0:
                ywtap = sch.add("sync", "do",
                                dma([(taps["yw"], Ef[:, YW0:YW0 + 1024])]),
                                waits=[ydone])

            bn_a = sch.add("vector", "vv",
                           lambda eng: eng.reduce_sum(STAT[:, 20:21],
                                                      Ef[:, YW0:YW0 + 1024], AX),
                           waits=[ydone])
            bn_b = sch.add("vector", "vv",
                           lambda eng: eng.tensor_single_scalar(
                               STAT[:, ST_MEAN:ST_MEAN + 1], STAT[:, 20:21],
                               1.0 / D, ALU.mult), waits=[bn_a])

            def mkbn_c(eng):
                eng.tensor_scalar(Ef[:, X2W0:X2W0 + 1024], Ef[:, YW0:YW0 + 1024],
                                  STAT[:, ST_MEAN:ST_MEAN + 1], 0.0,
                                  ALU.subtract, ALU.add)
                eng.tensor_mul(Ef[:, YW0:YW0 + 1024], Ef[:, X2W0:X2W0 + 1024],
                               Ef[:, X2W0:X2W0 + 1024])
                return eng.reduce_sum(STAT[:, 21:22], Ef[:, YW0:YW0 + 1024], AX)
            bn_c = sch.add("vector", "vv", mkbn_c,
                           waits=[bn_b] + ([ywtap] if ywtap else []))
            bn = sch.add("vector", "vv",
                         lambda eng: eng.tensor_single_scalar(
                             STAT[:, ST_VAR:ST_VAR + 1], STAT[:, 21:22],
                             1.0 / D, ALU.mult), waits=[bn_c])

            def mkrstd(eng):
                eng.activation(STAT[:, ST_RSTD:ST_RSTD + 1],
                               STAT[:, ST_VAR:ST_VAR + 1], ACT.Ln,
                               bias=scal[:, C_TINY:C_TINY + 1],
                               scale=scal[:, C_ONE:C_ONE + 1])
                return eng.activation(STAT[:, ST_RSTD:ST_RSTD + 1],
                                      STAT[:, ST_RSTD:ST_RSTD + 1], ACT.Exp,
                                      bias=scal[:, C_ZERO:C_ZERO + 1],
                                      scale=scal[:, C_NHALF:C_NHALF + 1])
            sttap = None
            if "stt" in taps and ti == 0:
                sttap = sch.add("sync", "do", dma([(taps["stt"], STAT[:, :])]),
                                waits=[bn])
                sch.add("sync", "do", dma([(taps["x2w"], Ef[:, X2W0:X2W0 + 1024])]),
                        waits=[bn]) if "x2w" in taps else None
            rstd = sch.add("scalar", "s2", mkrstd,
                           waits=[bn] + ([sttap] if sttap else []))

            def mkx2(eng):
                eng.tensor_scalar(Ef[:, X2W0:X2W0 + 1024], Ef[:, X2W0:X2W0 + 1024],
                                  STAT[:, ST_RSTD:ST_RSTD + 1], 0.0,
                                  ALU.mult, ALU.add)
                return eng.reduce_max(STAT[:, 20:21], Ef[:, X2W0:X2W0 + 1024],
                                      AX, apply_absolute_value=True)
            x2done = sch.add("vector", "vv", mkx2,
                             waits=[rstd] + ([lnst[ti - 1]["spx2"]] if ti >= 1 else []))
            x2done = sch.add("vector", "vv",
                             lambda eng: eng.tensor_single_scalar(
                                 STAT[:, ST_MX:ST_MX + 1], STAT[:, 20:21],
                                 1e-5, ALU.max), waits=[x2done])

            def mks1(eng):
                eng.activation(STAT[:, ST_S1:ST_S1 + 1], STAT[:, ST_MX:ST_MX + 1],
                               ACT.Ln)
                return eng.activation(STAT[:, ST_S1:ST_S1 + 1],
                                      STAT[:, ST_S1:ST_S1 + 1], ACT.Exp,
                                      bias=scal[:, C_LN127:C_LN127 + 1],
                                      scale=scal[:, C_NEG1:C_NEG1 + 1])
            s1 = sch.add("scalar", "s2", mks1, waits=[x2done])

            def mkg1b(eng):
                eng.tensor_scalar(Ef[:, YW0:YW0 + 1024], Ef[:, X2W0:X2W0 + 1024],
                                  STAT[:, ST_S1:ST_S1 + 1], MAGIC, ALU.mult, ALU.add)
                return eng.tensor_single_scalar(Dd[:, G1T0:G1T0 + 1024],
                                                Ef[:, YW0:YW0 + 1024], MAGIC,
                                                ALU.subtract)
            g1done = sch.add("vector", "vv", mkg1b,
                             waits=[s1] + ([ywtap] if ywtap else []))

            spx2 = sch.add("sync", "do", dma(
                [(d_x2sp[ti * 128:(ti + 1) * 128, :], Ef[:, X2W0:X2W0 + 1024])]),
                waits=[x2done])

            wg = [g1done] + ([lnst[ti - 1]["g1ev"]] if ti >= 1 else [])

            def mkg1t(eng):
                last = None
                for dc in range(8):
                    last = eng.transpose(PSB[:, dc * 128:(dc + 1) * 128],
                                         Dd[:, G1T0 + dc * 128:G1T0 + (dc + 1) * 128],
                                         cstb[:, IDB:IDB + 128])
                return last
            g1t = sch.add("tensor", "tt", mkg1t, waits=wg)

            def mkmxr(eng):
                return eng.matmul(PS3[0:1, 0:128], STAT[:, ST_MX:ST_MX + 1],
                                  cst[:, IDF:IDF + 128], start=True, stop=True)
            mxr = sch.add("tensor", "tt", mkmxr,
                          waits=[x2done] + ([lnst[ti - 1]["mxev"]] if ti >= 1 else []))

            def mkg1e(eng):
                return eng.tensor_copy(Dd[:, G1S0:G1S0 + 1024], PSB[:, 0:1024])
            g1ev = sch.add("vector", "vv", mkg1e,
                           waits=[g1t] + ([lnst[ti - 1]["spg1"]] if ti >= 1 else []))

            def mkmxe(ti=ti):
                def fn(eng):
                    return eng.tensor_copy(Dd[0:1, MXR0 + ti * 128:MXR0 + (ti + 1) * 128],
                                           PS3[0:1, 0:128])
                return fn
            mxev = sch.add("vector", "vv", mkmxe(), waits=[mxr])

            spg1 = sch.add("sync", "do", dma(
                [(d_g1sp[dc, :, ti * 128:(ti + 1) * 128],
                  Dd[:, G1S0 + dc * 128:G1S0 + (dc + 1) * 128]) for dc in range(8)]),
                waits=[g1ev], n_inc=1)

            lnst[ti] = {"ydone": ydone, "x2done": x2done, "g1ev": g1ev,
                        "mxev": mxev, "spx2": spx2, "spg1": spg1}
            prev_wo_evac = [ydone]

            for it in range(ti * 8, min((ti + 1) * 8, len(pa_items))):
                nm_, t_, s_, col_ = pa_items[it]
                dram_ = d_w1 if nm_ == "w1" else d_w2
                slot = (it % 2) * 512
                wld = [pa_reds[it - 2]] if it >= 2 else list(p0_done)
                ld_ = sch.add("gpsimd", "dq", dma(
                    [(SROW[:, slot:slot + 512],
                      dram_[t_ * 128:(t_ + 1) * 128,
                            s_ * 512:(s_ + 1) * 512])]), waits=wld)

                def mkpar(s_=s_, col_=col_, slot=slot):
                    def fn(eng):
                        dst = (STAT[:, ST_W + col_:ST_W + col_ + 1] if s_ == 0
                               else STAT[:, 18 + slot // 512:19 + slot // 512])
                        return eng.reduce_sum(dst, SROW[:, slot:slot + 512], AX,
                                              apply_absolute_value=True)
                    return fn
                red_ = sch.add("vector", "vv", mkpar(), waits=[ld_])
                pa_reds.append(red_)
                last_ = red_
                if s_ != 0:
                    def mkpaa(col_=col_, slot=slot):
                        def fn(eng):
                            return eng.tensor_add(
                                STAT[:, ST_W + col_:ST_W + col_ + 1],
                                STAT[:, ST_W + col_:ST_W + col_ + 1],
                                STAT[:, 18 + slot // 512:19 + slot // 512])
                        return fn
                    last_ = sch.add("vector", "vv", mkpaa(), waits=[red_])
                pa_steps.append(last_)
                if it == 63:
                    pa_fin["w1"] = last_
                if it == 127:
                    pa_fin["w2"] = last_

    def mkmxcast(eng):
        eng.tensor_copy(SROW[0:1, 0:1024], Dd[0:1, MXR0:MXR0 + 1024])
        eng.tensor_copy(SROW[32:33, 0:1024], Dd[0:1, MXR0 + 1024:MXR0 + 2048])
        return eng.tensor_copy(SROW[64:65, 0:LQ - 2048],
                               Dd[0:1, MXR0 + 2048:MXR0 + LQ])
    mxcast = sch.add("vector", "vv", mkmxcast,
                     waits=[lnst[QTN - 1]["mxev"]])
    spmx = sch.add("sync", "do", dma(
        [(d_mxsp[0:1, 0:1024], SROW[0:1, 0:1024]),
         (d_mxsp[0:1, 1024:2048], SROW[32:33, 0:1024]),
         (d_mxsp[0:1, 2048:LQ], SROW[64:65, 0:LQ - 2048])]), waits=[mxcast])
    if "mx" in taps:
        sch.add("sync", "do", dma(
            [(taps["mx"][0:1, 0:1024], SROW[0:1, 0:1024]),
             (taps["mx"][0:1, 1024:2048], SROW[32:33, 0:1024]),
             (taps["mx"][0:1, 2048:LQ], SROW[64:65, 0:LQ - 2048])]), waits=[mxcast])

    if "x2" in taps:
        sch.add("sync", "do", dma([(taps["x2"], d_x2sp)]),
                waits=[lnst[ti]["spx2"] for ti in range(QTN)])
    if "g1" in taps:
        sch.add("gpsimd", "do", dma([(taps["g1"], d_g1sp)]),
                waits=[lnst[ti]["spg1"] for ti in range(QTN)])
    p1_done = [lnst[QTN - 1]["g1ev"], mxcast, uev[-1]]
    p1_sp = [lnst[ti]["spx2"] for ti in range(QTN)] + \
            [lnst[ti]["spg1"] for ti in range(QTN)] + [spmx]

    WSTG2 = Df
    CWB0 = 256 + NPAIR * 128
    st_cw = sch.add("sync", "dw", dma_slow(
        [(cst[:, CWB0:CWB0 + 32],
          bass.AP(d_cw.tensor, 0, [[3, 128], [384, 32], [1, 1]])),
         (cst[:, CWB0 + 32:CWB0 + 64], bass.AP(d_cw.tensor, 1, [[3, 128], [384, 32], [1, 1]])),
         (cst[:, CWB0 + 64:CWB0 + 96], bass.AP(d_cw.tensor, 2, [[3, 128], [384, 32], [1, 1]])),
         (cst[:, CWB0 + 96:CWB0 + 128], bass.AP(d_cb.tensor, 0, [[1, 128], [128, 32]]))]),
        waits=p1_done)
    CW0, CW1, CW2, CB0 = CWB0, CWB0 + 32, CWB0 + 64, CWB0 + 96

    def wquant(dram, ntiles, row_free, outT_sel, c_sw, c_mr, waits0,
               stat_ofs, rs_col, pa_last):
        nsub = row_free // 1024

        def mkrs(eng):
            return eng.reduce_sum(STAT[:, rs_col:rs_col + 1],
                                  STAT[:, ST_W + stat_ofs:
                                       ST_W + stat_ofs + ntiles], AX)
        rs = sch.add("vector", "vq", mkrs, waits=[pa_last])
        tot = sch.add("tensor", "tq",
                      lambda eng: eng.matmul(PS3[0:1, 0:1],
                                             STAT[:, rs_col:rs_col + 1],
                                             scal[:, C_ONE:C_ONE + 1],
                                             start=True, stop=True),
                      waits=[rs, st_scal, lnst[QTN - 1]["mxev"], spmx])

        def mksw(eng):
            eng.activation(SROW[0:1, 641:642], PS3[0:1, 0:1], ACT.Ln)
            return eng.activation(SROW[0:1, 642:643], SROW[0:1, 641:642], ACT.Exp,
                                  bias=scal[0:1, C_X1:C_X1 + 1],
                                  scale=scal[0:1, C_NEG1:C_NEG1 + 1])
        sw = sch.add("scalar", "sq", mksw, waits=[tot])

        def mkhl_a(eng):
            eng.tensor_single_scalar(SROW[0:1, 645:646], SROW[0:1, 642:643],
                                     1e5, ALU.min)
            return eng.tensor_copy(SROW[0:1, 640:641], PS3[0:1, 0:1])
        hl_a = sch.add("vector", "vq", mkhl_a, waits=[sw])

        def mkhl_b(eng):
            eng.tensor_copy(E[0:1, 0:1], SROW[0:1, 645:646])
            return eng.tensor_copy(E[0:1, 2:3], SROW[0:1, 640:641])
        hl_b = sch.add("vector", "vq", mkhl_b, waits=[hl_a])

        def mkhl_c(eng):
            eng.tensor_sub(SROW[0:1, 643:644], SROW[0:1, 645:646], E[0:1, 0:1])
            return eng.tensor_sub(SROW[0:1, 644:645], SROW[0:1, 640:641],
                                  E[0:1, 2:3])
        hl_c = sch.add("vector", "vq", mkhl_c, waits=[hl_b])

        def mkhl_d(eng):
            eng.tensor_copy(E[0:1, 1:2], SROW[0:1, 643:644])
            return eng.tensor_copy(E[0:1, 3:4], SROW[0:1, 644:645])
        hl = sch.add("vector", "vq", mkhl_d, waits=[hl_c])

        def mkbc(eng):
            eng.matmul(PS3[:, 0:1], cstb[0:1, ONB:ONB + 128], E[0:1, 0:1],
                       start=True, stop=False)
            eng.matmul(PS3[:, 0:1], cstb[0:1, ONB:ONB + 128], E[0:1, 1:2],
                       start=False, stop=True)
            eng.matmul(PS3[:, 1:2], cstb[0:1, ONB:ONB + 128], E[0:1, 2:3],
                       start=True, stop=False)
            return eng.matmul(PS3[:, 1:2], cstb[0:1, ONB:ONB + 128], E[0:1, 3:4],
                              start=False, stop=True)
        bc = sch.add("tensor", "tq", mkbc, waits=[hl])

        def mksc_a(eng):
            eng.tensor_copy(scal[:, c_sw:c_sw + 1], PS3[:, 0:1])
            return eng.tensor_scalar(STAT[:, 22:23], PS3[:, 1:2],
                                     1.0 / (DH * D), 1e-5, ALU.mult, ALU.max)
        sc_a = sch.add("vector", "vq", mksc_a, waits=[bc])
        sc = sch.add("vector", "vq",
                     lambda eng: eng.tensor_single_scalar(
                         scal[:, c_mr:c_mr + 1], STAT[:, 22:23],
                         1.0 / 127.0, ALU.mult), waits=[sc_a])

        tlast = None
        prev_tr_ev = None
        for t in range(ntiles):
            buf = t % 2
            w = ([sc] + list(waits0)) if t < 2 else [tern_steps[t - 2]]
            ld = sch.add(*(("sync", "dq") if t % 2 == 0 else ("gpsimd", "dq2")), dma(
                [(WSTG2[:, buf * row_free:(buf + 1) * row_free],
                  dram[t * 128:(t + 1) * 128, :])]), waits=w)

            def mkq(t=t, buf=buf):
                def fn(eng):
                    last = None
                    for ss_ in range(nsub):
                        o = buf * row_free + ss_ * 1024
                        eng.tensor_scalar(SROW[:, 0:1024], WSTG2[:, o:o + 1024],
                                          scal[:, c_sw:c_sw + 1], MAGIC,
                                          ALU.mult, ALU.add)
                        eng.tensor_scalar(SROW[:, 0:1024], SROW[:, 0:1024],
                                          -MAGIC, -1.0, ALU.add, ALU.max)
                        last = eng.tensor_single_scalar(
                            E[:, (ss_ % 8) * 1024:(ss_ % 8 + 1) * 1024]
                            if nsub > 1 else E[:, (t % 8) * 1024:(t % 8 + 1) * 1024],
                            SROW[:, 0:1024], 1.0, ALU.min)
                    return last
                return fn
            qs = sch.add("vector", "vq", mkq(), waits=[ld] +
                         ([prev_tr_ev] if prev_tr_ev else []))

            ngrp = (row_free // 128)
            grp_ev = None
            for g0 in range(0, ngrp, 8):
                def mktr(t=t, g0=g0, nsub=nsub):
                    def fn(eng):
                        last = None
                        for j in range(min(8, ngrp - g0)):
                            cc = g0 + j
                            src_slot = (cc // 8) * 1024 if nsub > 1 else (t % 8) * 1024
                            src_off = (cc % 8) * 128 if nsub > 1 else cc * 128
                            last = eng.transpose(
                                PSB[:, j * 128:(j + 1) * 128],
                                E[:, src_slot + src_off:src_slot + src_off + 128],
                                cstb[:, IDB:IDB + 128])
                        return last
                    return fn
                wtr = [qs] + ([grp_ev] if grp_ev else ([prev_tr_ev] if prev_tr_ev else []))
                tr = sch.add("tensor", "tq", mktr(), waits=wtr)

                def mkev(t=t, g0=g0):
                    def fn(eng):
                        last = None
                        for j in range(min(8, ngrp - g0)):
                            cc = g0 + j
                            dst, off = outT_sel(t, cc)
                            last = eng.tensor_copy(dst[:, off:off + 128],
                                                   PSB[:, j * 128:(j + 1) * 128])
                        return last
                    return fn
                grp_ev = sch.add("vector", "vq", mkev(), waits=[tr])
            prev_tr_ev = grp_ev
            tern_steps.append(grp_ev)
            tlast = grp_ev
        return tlast

    def t1_sel(t, cc):
        dst = A if cc < 4 else Bb
        return dst, (cc % 4) * 4096 + t * 128

    def t2_sel(t, cc):
        return C, cc * 1024 + t * 128

    tern_steps = []
    t1_last = wquant(d_w1, 32, 1024, t1_sel, C_SW1, C_M1R,
                     waits0=list(p1_done) + p1_sp,
                     stat_ofs=0, rs_col=22, pa_last=pa_fin["w1"])
    tern_steps = []
    t2_last = wquant(d_w2, 8, 4096, t2_sel, C_SW2, C_M2R, waits0=[t1_last],
                     stat_ofs=32, rs_col=23, pa_last=pa_fin["w2"])

    p2_done = [t1_last, t2_last, st_cw]
    if "t1" in taps:
        sch.add("gpsimd", "do", dma([(taps["t1"], A[:, 0:4096])]), waits=[t1_last])
    if "t2" in taps:
        sch.add("gpsimd", "do", dma([(taps["t2"], C[:, 0:1024])]), waits=p2_done)

    CONVO = Df
    G1SL0 = 0
    H1S0 = 2064
    H1SB0 = 5700
    T0B0 = 4096
    ABSB0 = 4608
    RRB0 = 2324
    S2HI0, S2LO0 = 2582, 2838
    RBCf0 = 1547
    S2Ef0 = 1805
    YW3f0 = 2061
    MMXf0 = 2317
    X2SL0, OUTW0, Y2W0 = 0, 2048, 3072
    R_ROW0 = 640

    prev_out = None
    prev_h1last = None
    prev_rbce = None
    prev_w2ev = None
    w2evs = []
    prev_s2e = None
    prev_mxuse = None
    out_steps = []
    for ci, (t0, tw) in enumerate(FFN_CHUNKS):
        hw = tw + 2
        g1c0 = t0 - 1
        lo_pad = (t0 == 0)
        hi_pad = (t0 + tw == LQ)
        ld_lo = max(0, g1c0)
        ld_hi = min(LQ, g1c0 + hw)
        ldw = ld_hi - ld_lo
        dst0 = ld_lo - g1c0

        w = list(p2_done) + p1_sp if ci == 0 else [prev_h1last]
        g1ld = sch.add("sync", "dq", dma(
            [(E[:, G1SL0 + dc * hw + dst0:G1SL0 + dc * hw + dst0 + ldw],
              d_g1sp[dc, :, ld_lo:ld_hi]) for dc in range(8)]), waits=w)
        mxld = sch.add("sync", "dq", dma(
            [(SROW[96:97, 0:ldw], d_mxsp[0:1, ld_lo:ld_hi])]),
            waits=([spmx] + list(p2_done)) if ci == 0 else
                  [prev_rbce, prev_mxuse])
        padsteps = [g1ld]
        if lo_pad or hi_pad:
            def mkpad(lo_pad=lo_pad, hi_pad=hi_pad, hw=hw):
                def fn(eng):
                    last = None
                    for dc in range(8):
                        if lo_pad:
                            last = eng.memset(E[:, G1SL0 + dc * hw:G1SL0 + dc * hw + 1], 0.0)
                        if hi_pad:
                            last = eng.memset(
                                E[:, G1SL0 + dc * hw + hw - 1:G1SL0 + dc * hw + hw], 0.0)
                    return last
                return fn
            padsteps.append(sch.add("vector", "vf", mkpad(), waits=[g1ld]))

        x2lds = []
        for sub in range(tw // 128):
            x2lds.append(sch.add("sync", "dr", dma(
                [(WOTf[:, X2SL0 + sub * 1024:X2SL0 + (sub + 1) * 1024],
                  d_x2sp[t0 + sub * 128:t0 + (sub + 1) * 128, :])]),
                waits=[prev_out] if prev_out else list(p2_done)))

        def mkrr_a(hw=hw, dst0=dst0, ldw=ldw):
            def fn(eng):
                eng.memset(SROW[0:1, R_ROW0:R_ROW0 + hw], 0.0)
                return eng.tensor_scalar(
                    SROW[0:1, R_ROW0 + dst0:R_ROW0 + dst0 + ldw],
                    SROW[96:97, 0:ldw], scal[0:1, C_M1R:C_M1R + 1], 0.0,
                    ALU.mult, ALU.add)
            return fn
        rrow_a = sch.add("vector", "vf", mkrr_a(),
                         waits=[mxld] + ([prev_rbce] if prev_rbce else []))
        rrow = sch.add("vector", "vf",
                       (lambda hw=hw: (lambda eng: eng.tensor_copy(
                           E[0:1, RRB0:RRB0 + hw],
                           SROW[0:1, R_ROW0:R_ROW0 + hw])))(),
                       waits=[rrow_a])
        prev_mxuse = rrow

        def mkrbc(hw=hw):
            def fn(eng):
                return eng.matmul(PS3[:, 0:hw], cstb[0:1, ONB:ONB + 128],
                                  E[0:1, RRB0:RRB0 + hw], start=True, stop=True)
            return fn
        rbc = sch.add("tensor", "tf", mkrbc(),
                      waits=[rrow] + ([prev_s2e] if prev_s2e else []))

        def mkrbce(hw=hw):
            def fn(eng):
                return eng.tensor_copy(Ef[:, RBCf0:RBCf0 + hw], PS3[:, 0:hw])
            return fn
        rbce = sch.add("vector", "vf", mkrbce(), waits=[rbc])
        prev_rbce = rbce

        h1evs, conv_steps, ct0s = [], [], []
        pend_cv = None
        for h in range(32):
            wmm = list(padsteps) if h == 0 else []
            if ci == 0 and h == 0:
                wmm += p2_done
            if h == 0 and prev_w2ev is not None:
                wmm.append(prev_w2ev)
            if h >= 2:
                wmm.append(h1evs[h - 2])

            def mkw1(h=h, hw=hw):
                def fn(eng):
                    ps = (PS0 if h % 2 == 0 else PS1)[:, 0:hw]
                    last = None
                    for dc in range(8):
                        src = A if dc < 4 else Bb
                        last = eng.matmul(
                            ps, src[:, (dc % 4) * 4096 + h * 128:(dc % 4) * 4096 + (h + 1) * 128],
                            E[:, G1SL0 + dc * hw:G1SL0 + (dc + 1) * hw],
                            start=(dc == 0), stop=(dc == 7))
                    return last
                return fn
            mm = sch.add("tensor", "tf", mkw1(), waits=wmm)

            hb = H1S0 if h % 2 == 0 else H1SB0

            def mkh1(h=h, hw=hw, hb=hb):
                def fn(eng):
                    return eng.tensor_mul(E[:, hb:hb + hw],
                                          (PS0 if h % 2 == 0 else PS1)[:, 0:hw],
                                          Ef[:, RBCf0:RBCf0 + hw])
                return fn
            wv = [mm, rbce] + ([conv_steps[h - 2]] if h >= 2 else [])
            h1ev = sch.add("vector", "vf", mkh1(), waits=wv)
            h1evs.append(h1ev)

            tb = T0B0 + (h % 2) * 258

            def mkct0(h=h, tw=tw, hb=hb, tb=tb):
                def fn(eng):
                    return eng.activation(Ef[:, tb:tb + tw], E[:, hb:hb + tw],
                                          ACT.Copy, bias=0.0,
                                          scale=cst[:, CW0 + h:CW0 + h + 1])
                return fn
            ct0 = sch.add("scalar", "sf", mkct0(), waits=[h1ev, st_cw])
            ct0s.append(ct0)

            if pend_cv is not None:
                conv_steps.append(sch.add("vector", "vf", pend_cv[0],
                                          waits=pend_cv[1]))

            def mkcv(h=h, tw=tw, hb=hb, tb=tb):
                def fn(eng):
                    o = h * 256
                    eng.scalar_tensor_tensor(Ef[:, YW3f0:YW3f0 + tw],
                                             E[:, hb + 1:hb + 1 + tw],
                                             cst[:, CW1 + h:CW1 + h + 1],
                                             Ef[:, tb:tb + tw],
                                             ALU.mult, ALU.add)
                    return eng.scalar_tensor_tensor(CONVO[:, o:o + tw],
                                                    E[:, hb + 2:hb + 2 + tw],
                                                    cst[:, CW2 + h:CW2 + h + 1],
                                                    Ef[:, YW3f0:YW3f0 + tw],
                                                    ALU.mult, ALU.add)
                return fn
            pend_cv = (mkcv(), [ct0])
        conv_steps.append(sch.add("vector", "vf", pend_cv[0], waits=pend_cv[1]))
        prev_h1last = h1evs[-1]

        def mksilu(tw=tw):
            def fn(eng):
                last = None
                for h in range(32):
                    o = h * 256
                    last = eng.activation(CONVO[:, o:o + tw], CONVO[:, o:o + tw],
                                          ACT.Silu,
                                          bias=cst[:, CB0 + h:CB0 + h + 1],
                                          scale=scal[:, C_ONE:C_ONE + 1])
                return last
            return fn
        silu = sch.add("scalar", "sf", mksilu(), waits=[conv_steps[-1]])
        if "a1" in taps and ci == 0:
            sch.add("sync", "do", dma([(taps["a1"], CONVO[:, 0:8192])]), waits=[silu])

        absteps = []
        mxacc = None
        for h in range(32):
            def mkabs(h=h, tw=tw):
                def fn(eng):
                    return eng.activation(Ef[:, ABSB0 + (h % 2) * 256:
                                             ABSB0 + (h % 2) * 256 + tw],
                                          CONVO[:, h * 256:h * 256 + tw], ACT.Abs)
                return fn
            wab = [silu] if h < 2 else [silu, absteps[h - 2][1]]
            ab = sch.add("scalar", "sf", mkabs(), waits=wab)

            def mkmx(h=h, tw=tw):
                def fn(eng):
                    src = Ef[:, ABSB0 + (h % 2) * 256:ABSB0 + (h % 2) * 256 + tw]
                    if h == 0:
                        return eng.tensor_copy(Ef[:, MMXf0:MMXf0 + tw], src)
                    return eng.tensor_max(Ef[:, MMXf0:MMXf0 + tw],
                                          Ef[:, MMXf0:MMXf0 + tw], src)
                return fn
            mx_ = sch.add("vector", "vf", mkmx(), waits=[ab])
            absteps.append((ab, mx_))
            mxacc = mx_

        def mkmxt(tw=tw):
            def fn(eng):
                last = None
                for j in range(tw // 128):
                    last = eng.matmul(PS3[:, j * 128:(j + 1) * 128],
                                      Ef[:, MMXf0 + j * 128:MMXf0 + (j + 1) * 128],
                                      cst[:, IDF:IDF + 128], start=True, stop=True)
                return last
            return fn
        mxt = sch.add("tensor", "tf", mkmxt(), waits=[mxacc])

        def mkmxred_b(tw=tw):
            def fn(eng):
                last = None
                for j in range(tw // 128):
                    last = eng.reduce_max(STAT[:, 30 + j:31 + j],
                                          PS3[:, j * 128:(j + 1) * 128], AX)
                return last
            return fn
        mxred_b = sch.add("vector", "vf", mkmxred_b(), waits=[mxt])

        def mkmxred_c(tw=tw):
            def fn(eng):
                last = None
                for j in range(tw // 128):
                    last = eng.tensor_single_scalar(STAT[:, 28 + j:29 + j],
                                                    STAT[:, 30 + j:31 + j],
                                                    1e-5, ALU.max)
                return last
            return fn
        mxred = sch.add("vector", "vf", mkmxred_c(), waits=[mxred_b])

        def mkr2(tw=tw):
            def fn(eng):
                last = None
                for j in range(tw // 128):
                    last = eng.tensor_scalar(STAT[:, 18 + j:19 + j],
                                             STAT[:, 28 + j:29 + j],
                                             scal[:, C_M2R:C_M2R + 1], 0.0,
                                             ALU.mult, ALU.add)
                return last
            return fn
        r2c = sch.add("vector", "vf", mkr2(), waits=[mxred])

        def mks2(tw=tw):
            def fn(eng):
                last = None
                for j in range(tw // 128):
                    eng.activation(STAT[:, 28 + j:29 + j], STAT[:, 28 + j:29 + j],
                                   ACT.Ln)
                    last = eng.activation(STAT[:, 28 + j:29 + j],
                                          STAT[:, 28 + j:29 + j], ACT.Exp,
                                          bias=scal[:, C_LN127:C_LN127 + 1],
                                          scale=scal[:, C_NEG1:C_NEG1 + 1])
                return last
            return fn
        s2c = sch.add("scalar", "sf", mks2(), waits=[r2c])

        def mks2t(tw=tw):
            def fn(eng):
                last = None
                for j in range(tw // 128):
                    last = eng.matmul(PS3[0:1, j * 128:(j + 1) * 128],
                                      STAT[:, 28 + j:29 + j],
                                      cst[:, IDF:IDF + 128], start=True, stop=True)
                return last
            return fn
        s2t = sch.add("tensor", "tf", mks2t(), waits=[s2c])

        def mks2hl_a(tw=tw):
            def fn(eng):
                eng.tensor_copy(SROW[0:1, 0:tw], PS3[0:1, 0:tw])
                return eng.tensor_copy(E[0:1, S2HI0:S2HI0 + tw], PS3[0:1, 0:tw])
            return fn
        s2hl_a = sch.add("vector", "vf", mks2hl_a(), waits=[s2t])

        def mks2hl_b(tw=tw):
            def fn(eng):
                return eng.tensor_sub(SROW[32:33, 0:tw], SROW[0:1, 0:tw],
                                      E[0:1, S2HI0:S2HI0 + tw])
            return fn
        s2hl_b = sch.add("vector", "vf", mks2hl_b(), waits=[s2hl_a])
        s2hl = sch.add("vector", "vf",
                       (lambda tw=tw: (lambda eng: eng.tensor_copy(
                           E[0:1, S2LO0:S2LO0 + tw], SROW[32:33, 0:tw])))(),
                       waits=[s2hl_b])

        def mks2bc(tw=tw):
            def fn(eng):
                eng.matmul(PS3[:, 0:tw], cstb[0:1, ONB:ONB + 128],
                           E[0:1, S2HI0:S2HI0 + tw], start=True, stop=False)
                return eng.matmul(PS3[:, 0:tw], cstb[0:1, ONB:ONB + 128],
                                  E[0:1, S2LO0:S2LO0 + tw], start=False, stop=True)
            return fn
        s2bc = sch.add("tensor", "tf", mks2bc(), waits=[s2hl])

        def mks2e(tw=tw):
            def fn(eng):
                return eng.tensor_copy(Ef[:, S2Ef0:S2Ef0 + tw], PS3[:, 0:tw])
            return fn
        s2e = sch.add("vector", "vf", mks2e(), waits=[s2bc])
        prev_s2e = s2e
        if "s2" in taps and ci == 0:
            sch.add("sync", "do", dma([(taps["s2"], Ef[:, S2Ef0:S2Ef0 + 256])]),
                    waits=[s2e])

        def mkg2(tw=tw):
            def fn(eng):
                last = None
                for h in range(32):
                    o = h * 256
                    eng.tensor_mul(Ef[:, YW3f0:YW3f0 + tw], CONVO[:, o:o + tw],
                                   Ef[:, S2Ef0:S2Ef0 + tw])
                    last = eng.tensor_scalar(Dd[:, h * 512:h * 512 + tw],
                                             Ef[:, YW3f0:YW3f0 + tw],
                                             MAGIC, -MAGIC, ALU.add, ALU.add)
                return last
            return fn
        g2 = sch.add("vector", "vf", mkg2(), waits=[s2e])
        if "g2" in taps and ci == 0:
            sch.add("gpsimd", "do", dma(
                [(taps["g2"][:, h * 256:h * 256 + 256], Dd[:, h * 512:h * 512 + 256])
                 for h in range(32)]), waits=[g2])

        for sub in range(tw // 128):
            gi = len(w2evs)
            co = (gi % 2) * 512

            def mkw2(sub=sub, co=co):
                def fn(eng):
                    last = None
                    for half in range(2):
                        ps = (PS0 if half == 0 else PS1)[:, co:co + 512]
                        for h in range(32):
                            last = eng.matmul(
                                ps, Dd[:, h * 512 + sub * 128:h * 512 + (sub + 1) * 128],
                                C[:, h * 1024 + half * 512:h * 1024 + (half + 1) * 512],
                                start=(h == 0), stop=(h == 31))
                    return last
                return fn
            w2w = [g2] + ([w2evs[gi - 2]] if gi >= 2 else [])
            w2mm = sch.add("tensor", "tf", mkw2(), waits=w2w)

            def mky2(sub=sub, co=co):
                def fn(eng):
                    eng.scalar_tensor_tensor(
                        WOTf[:, Y2W0:Y2W0 + 512], PS0[:, co:co + 512],
                        STAT[:, 18 + sub:19 + sub],
                        WOTf[:, X2SL0 + sub * 1024:X2SL0 + sub * 1024 + 512],
                        ALU.mult, ALU.add)
                    return eng.scalar_tensor_tensor(
                        WOTf[:, Y2W0 + 512:Y2W0 + 1024], PS1[:, co:co + 512],
                        STAT[:, 18 + sub:19 + sub],
                        WOTf[:, X2SL0 + sub * 1024 + 512:X2SL0 + (sub + 1) * 1024],
                        ALU.mult, ALU.add)
                return fn
            y2 = sch.add("vector", "vf", mky2(), waits=[w2mm, x2lds[sub]])
            w2evs.append(y2)
            prev_w2ev = y2

            bn2a = sch.add("vector", "vf",
                           lambda eng: eng.reduce_sum(STAT[:, 20:21],
                                                      WOTf[:, Y2W0:Y2W0 + 1024], AX),
                           waits=[y2])
            bn2b = sch.add("vector", "vf",
                           lambda eng: eng.tensor_single_scalar(
                               STAT[:, ST_MEAN:ST_MEAN + 1], STAT[:, 20:21],
                               1.0 / D, ALU.mult), waits=[bn2a])

            def mkbn2_c(eng):
                eng.tensor_scalar(WOTf[:, OUTW0:OUTW0 + 1024],
                                  WOTf[:, Y2W0:Y2W0 + 1024],
                                  STAT[:, ST_MEAN:ST_MEAN + 1], 0.0,
                                  ALU.subtract, ALU.add)
                eng.tensor_mul(WOTf[:, Y2W0:Y2W0 + 1024],
                               WOTf[:, OUTW0:OUTW0 + 1024],
                               WOTf[:, OUTW0:OUTW0 + 1024])
                return eng.reduce_sum(STAT[:, 21:22], WOTf[:, Y2W0:Y2W0 + 1024], AX)
            bn2c = sch.add("vector", "vf", mkbn2_c, waits=[bn2b])
            bn2 = sch.add("vector", "vf",
                          lambda eng: eng.tensor_single_scalar(
                              STAT[:, ST_VAR:ST_VAR + 1], STAT[:, 21:22],
                              1.0 / D, ALU.mult), waits=[bn2c])

            def mkrstd2(eng):
                eng.activation(STAT[:, ST_RSTD:ST_RSTD + 1],
                               STAT[:, ST_VAR:ST_VAR + 1], ACT.Ln,
                               bias=scal[:, C_TINY:C_TINY + 1],
                               scale=scal[:, C_ONE:C_ONE + 1])
                return eng.activation(STAT[:, ST_RSTD:ST_RSTD + 1],
                                      STAT[:, ST_RSTD:ST_RSTD + 1], ACT.Exp,
                                      bias=scal[:, C_ZERO:C_ZERO + 1],
                                      scale=scal[:, C_NHALF:C_NHALF + 1])
            rstd2 = sch.add("scalar", "sf", mkrstd2, waits=[bn2])

            def mkoutn(eng):
                return eng.tensor_scalar(WOTf[:, OUTW0:OUTW0 + 1024],
                                         WOTf[:, OUTW0:OUTW0 + 1024],
                                         STAT[:, ST_RSTD:ST_RSTD + 1], 0.0,
                                         ALU.mult, ALU.add)
            outn = sch.add("vector", "vf", mkoutn,
                           waits=[rstd2] + ([out_steps[-1]] if out_steps else []))
            prev_out = outn

            out_steps.append(sch.add("sync", "do", dma(
                [(d_y[t0 + sub * 128:t0 + (sub + 1) * 128, :],
                  WOTf[:, OUTW0:OUTW0 + 1024])]), waits=[outn]))

    if _PREFIX is not None:
        sch.steps = sch.steps[:_PREFIX]
    do_total = sum(s.n_inc for s in sch.steps if s.sem == "do")

    def fin(eng):
        return eng.nop()
    last = sch.add("sync", "dz", fin, waits=[_FW("do", do_total)])

    sch.plan()
    with nc.Block() as blk:
        sch.emit(pr, blk)
    return pr, sch, taps


class _FW:
    def __init__(self, sem, count):
        self.sem = sem
        self.count = count


_BUILT = None


def _get_built():
    global _BUILT
    if _BUILT is None:
        _BUILT = build()
    return _BUILT


def kernel(L0_attn_x, L0_attn_wq, L0_attn_wk_a, L0_attn_wk_b, L0_attn_wv_a,
           L0_attn_wv_b, L0_attn_wo, L0_ffn_w1, L0_ffn_conv_w, L0_ffn_conv_b,
           L0_ffn_w2):
    pr, sch, taps = _get_built()
    consts = _host_consts()
    x = np.ascontiguousarray(np.asarray(L0_attn_x, dtype=np.float32))
    w = {
        "wq": np.ascontiguousarray(L0_attn_wq, dtype=np.float32),
        "wka": np.ascontiguousarray(L0_attn_wk_a, dtype=np.float32),
        "wkb": np.ascontiguousarray(L0_attn_wk_b, dtype=np.float32),
        "wva": np.ascontiguousarray(L0_attn_wv_a, dtype=np.float32),
        "wvb": np.ascontiguousarray(L0_attn_wv_b, dtype=np.float32),
        "wo": np.ascontiguousarray(L0_attn_wo, dtype=np.float32),
        "w1": np.ascontiguousarray(L0_ffn_w1, dtype=np.float32),
        "cw": np.ascontiguousarray(
            np.asarray(L0_ffn_conv_w, dtype=np.float32).reshape(DH, 3)),
        "cb": np.ascontiguousarray(L0_ffn_conv_b, dtype=np.float32),
        "w2": np.ascontiguousarray(L0_ffn_w2, dtype=np.float32),
    }
    in_maps = []
    for c in range(8):
        b, half = divmod(c, 2)
        qlo = 0 if half == 0 else S - LQ
        m = dict(w)
        m["xq"] = np.ascontiguousarray(x[b, qlo:qlo + LQ, :])
        m["xf"] = np.ascontiguousarray(x[b])
        m.update(consts)
        in_maps.append(m)
    res = run_bass_kernel_spmd(pr.nc, in_maps, core_ids=list(range(8)))
    out = np.empty((B, S, D), dtype=np.float32)
    for c in range(8):
        b, half = divmod(c, 2)
        ownstart = 0 if half == 0 else 128
        y = np.asarray(res.results[c]["y"])
        out[b, half * 2048:(half + 1) * 2048, :] = y[ownstart:ownstart + 2048, :]
    if _TAPS:
        kernel.taps = [{k: np.asarray(res.results[c]["tap_" + k]) for k in _TAPS}
                       for c in range(8)]
    return out
